# version 10
# speedup vs baseline: 1.5048x; 1.1399x over previous
import contextlib
import numpy as np
import sys
sys.path.insert(0, '/opt/trn_rl_repo')
import concourse.bass as bass
import concourse.mybir as mybir
from concourse.bass_utils import run_bass_kernel_spmd

F32 = mybir.dt.float32
BF16 = mybir.dt.bfloat16
AF = mybir.ActivationFunctionType
ALU = mybir.AluOpType
AX = mybir.AxisListType


ATTNB_PHASE_LIMIT = 4
DEBUG_KTOT = False

def make_cfg(S=2048, B=4, sublayers=("attn_a", "ffn1", "attn_b", "ffn2")):
    return dict(S=S, B=B, D=1024, NT=S // 128, ND=8, DH=4096, NJ=32,
                TT=min(512, S), sublayers=tuple(sublayers))


class Sem:
    def __init__(self, handle, name):
        self.h, self.name, self.n = handle, name, 0


class Arena:
    def __init__(self, tensor, total_elems):
        self.t = tensor
        self.total = total_elems
        self.off = 0
        self.marks = []

    def push(self):
        self.marks.append(self.off)

    def pop(self):
        self.off = self.marks.pop()

    def alloc(self, elems_bf16, dtype=BF16):
        self.off = (self.off + 1) // 2 * 2
        start = self.off
        self.off += elems_bf16
        assert self.off <= self.total, f"SBUF arena overflow: {self.off} > {self.total}"
        view = self.t[:, start:start + elems_bf16]
        if dtype == F32:
            view = view.bitcast(F32)
        return view


class Prog:
    ENGINES = ("sync", "tensor", "vector", "scalar", "gpsimd")

    def __init__(self, nc):
        self.nc = nc
        self.streams = {e: [] for e in self.ENGINES}

    def op(self, eng, fn=None, waits=(), inc=None, amt=1):
        if inc is not None:
            inc.n += amt
        want = [(s.h, v) for (s, v) in waits if v > 0]
        ih = inc.h if inc is not None else None

        def emit(engine, fn=fn, want=want, ih=ih, amt=amt):
            for h, v in want:
                engine.wait_ge(h, v)
            if fn is not None:
                instr = fn(engine)
                if ih is not None:
                    instr.then_inc(ih, amt)
        self.streams[eng].append(emit)
        return inc.n if inc is not None else None

    def dma(self, eng, out, in_, waits=(), inc=None):
        return self.op(eng, lambda e: e.dma_start(out, in_), waits=waits, inc=inc, amt=16)

    def barrier(self, sems):
        snap = [(s, s.n) for s in sems if s.n > 0]
        for eng in self.ENGINES:
            self.op(eng, None, waits=snap)

    def emit_all(self, block):
        for eng in self.ENGINES:
            stream = self.streams[eng]

            def make(s):
                def _(engine):
                    for f in s:
                        f(engine)
                return _
            getattr(block, eng)(make(stream))


def build_program(cfg):
    S, B, D = cfg["S"], cfg["B"], cfg["D"]
    NT, ND, DH, NJ, TT = cfg["NT"], cfg["ND"], cfg["DH"], cfg["NJ"], cfg["TT"]
    NTT = S // TT
    SUBS = cfg["sublayers"]

    nc = bass.Bass("TRN2", target_bir_lowering=False, detect_race_conditions=True)
    dr = {}

    def din(name, shape, dtype=F32):
        dr[name] = nc.dram_tensor(name, shape, dtype, kind="ExternalInput").ap()
        return dr[name]

    din("x_in", [B, D, S])
    din("c_ident", [128, 128])
    if "attn_a" in SUBS:
        din("a_WqkvT", [D, 3 * D])
        din("a_WoT", [D, D])
        din("a_bq", [128, 8])
        din("a_bk", [128, 8])
        din("a_bvrow", [1, D])
        din("a_wq", [128, 1])
        din("a_wk", [128, 1])
        din("a_bo", [128, 8])
        din("c_rsum", [128, 2])
        din("c_sel2", [2, 128])
    if "attn_b" in SUBS:
        NF2 = S + 128
        din("b_WqkvT", [D, 3 * D])
        din("b_WoT", [D, D])
        din("b_bq", [128, 8])
        din("b_bkrow", [1, D])
        din("b_bvrow", [1, D])
        din("b_bo", [128, 8])
        din("b_Fc", [S, NF2])
        din("b_Fs", [S, NF2])
        din("b_Cr", [NF2, S])
        din("b_Ci", [NF2, S])
        din("b_kernT", [S, D])
        din("b_kDrow", [1, D])
        din("b_strips", [128, 128, S])
        din("b_Drep", [128, 128])
        din("c_diag", [128, 128])
        din("c_J", [128, 16])
    if "ffn1" in SUBS or "ffn2" in SUBS:
        din("f_WexpT", [DH, D])
        din("f_Wcon", [DH, D])
        din("f_selfU", [12, 128, NJ])
        din("f_otherU", [12, 128, NJ])
        din("f_selfD", [12, 128, NJ])
        din("f_otherD", [12, 128, NJ])
        din("f_repSU", [7, NJ, 128, 128])
        din("f_repOU", [7, NJ, 128, 128])
        din("f_repSD", [7, NJ, 128, 128])
        din("f_repOD", [7, NJ, 128, 128])
        din("f_bexp", [128, NJ])
        din("f_upbias", [128, NJ])
        din("f_bcon", [128, ND])
        din("f_dbias", [128, ND])

    y_out = nc.dram_tensor("y", [B, D, S], F32, kind="ExternalOutput").ap()
    dr["y"] = y_out
    x_res = nc.dram_tensor("x_res", [B, D, S], BF16, kind="Internal").ap()
    x_res2 = nc.dram_tensor("x_res2", [B, D, S], BF16, kind="Internal").ap()
    wexp_h = nc.dram_tensor("wexp_h", [128, ND * DH], BF16, kind="Internal").ap()
    wcon_h = nc.dram_tensor("wcon_h", [128, NJ * D], BF16, kind="Internal").ap()
    wqkv_h = nc.dram_tensor("wqkv_h", [128, ND * 3 * D], BF16, kind="Internal").ap()
    woT_h = nc.dram_tensor("woT_h", [128, ND * D], BF16, kind="Internal").ap()
    NF2 = S + 128
    wqkvB_h = nc.dram_tensor("wqkvB_h", [128, ND * 3 * D], BF16, kind="Internal").ap()
    woB_h = nc.dram_tensor("woB_h", [128, ND * D], BF16, kind="Internal").ap()
    Fc_h = nc.dram_tensor("Fc_h", [S, NF2], BF16, kind="Internal").ap()
    Fs_h = nc.dram_tensor("Fs_h", [S, NF2], BF16, kind="Internal").ap()
    Cr_h = nc.dram_tensor("Cr_h", [NF2, S], BF16, kind="Internal").ap()
    Ci_h = nc.dram_tensor("Ci_h", [NF2, S], BF16, kind="Internal").ap()
    Gr_h = nc.dram_tensor("Gr_h", [NF2, D], BF16, kind="Internal").ap()
    Gi_h = nc.dram_tensor("Gi_h", [NF2, D], BF16, kind="Internal").ap()
    Pr_h = nc.dram_tensor("Pr_h", [NF2, D], BF16, kind="Internal").ap()
    Pi_h = nc.dram_tensor("Pi_h", [NF2, D], BF16, kind="Internal").ap()
    q_h = nc.dram_tensor("q_h", [B, D, S], BF16, kind="Internal").ap()
    ktot_h = nc.dram_tensor("ktot_h", [B, S, D], BF16,
                            kind="ExternalOutput" if DEBUG_KTOT else "Internal").ap()
    v_h = nc.dram_tensor("v_h", [B, S, D], BF16, kind="Internal").ap()
    k_h = nc.dram_tensor("k_h", [B, S, D], BF16, kind="Internal").ap()
    yb_h = nc.dram_tensor("yb_h", [B, D, S], BF16, kind="Internal").ap()

    P = Prog(nc)
    es = contextlib.ExitStack()
    with es:
        E = es.enter_context
        sems = {}

        def sem(name):
            sems[name] = Sem(E(nc.semaphore(name)), name)
            return sems[name]

        sQ = sem("sQ")
        sG = sem("sG")
        sT = sem("sT")
        sV = sem("sV")
        sA = sem("sA")
        sO = sem("sO")
        sG2 = sem("sG2")
        sO2 = sem("sO2")
        sSs = [sem("sSs%d" % i) for i in range(4)]
        sFp = [sem("sFp0"), sem("sFp1")]
        sGp = [sem("sGp0"), sem("sGp1")]
        sCp = [sem("sCp0"), sem("sCp1")]
        sPp = [sem("sPp0"), sem("sPp1")]
        sLp = [sem("sLp0"), sem("sLp1")]
        sNp = [sem("sNp0"), sem("sNp1")]
        ALLSEMS = ([sQ, sG, sT, sV, sA, sO, sG2, sO2] + sFp + sGp + sCp + sPp
                   + sLp + sNp + sSs)

        pbank = [E(nc.psum_tensor(f"pb{i}", [128, 512], F32)) for i in range(8)]
        bank_free = {}

        def bwait(i):
            return [bank_free[i]] if i in bank_free else []

        def bfree(i, s):
            bank_free[i] = (s, s.n)

        AR_TOTAL = 106400
        arena_t = E(nc.sbuf_tensor("arena", [128, AR_TOTAL], BF16))
        AR = Arena(arena_t, AR_TOTAL)

        ident = AR.alloc(128)
        ones_colb = AR.alloc(2)[:, 0:1]
        ones_rowb_t = AR.alloc(128)
        ones_rowf_t = AR.alloc(256, F32)
        fb1 = AR.alloc(2 * NJ, F32)
        fbo = AR.alloc(2 * ND, F32)

        block = E(nc.Block())

        P.op("vector", lambda e: e.memset(ones_colb[:], 1.0), inc=sV)
        P.op("vector", lambda e: e.memset(ones_rowb_t[:], 1.0), inc=sV)
        P.op("vector", lambda e: e.memset(ones_rowf_t[:], 1.0), inc=sV)
        ones_rowb = ones_rowb_t[0:1, :]
        ones_rowf = ones_rowf_t[0:1, :]
        P.dma("gpsimd", ident[:], dr["c_ident"][:], inc=sG)
        P.barrier(ALLSEMS)

        def self_edge(s):
            return [(s, s.n)]

        cstore = AR.alloc(2 * 16, F32)
        cslot = [0]

        def constf(value):
            sl = cstore[:, cslot[0]:cslot[0] + 1]
            cslot[0] += 1
            P.op("vector", lambda e, sl=sl, value=value: e.memset(sl, float(value)), inc=sV)
            nc.const_aps.register(F32, float(value), sl)
            return value

        constf(1e-5)
        constf(-0.5)
        constf(1.0 / D)
        constf(1e-6)
        constf(1.0 / 64.0)
        constf(0.5)
        constf(-0.4)
        P.barrier(ALLSEMS)

        def ffn_prep():
            AR.push()
            DT = 256
            wj_a = AR.alloc(NJ * DT)
            wj_b = AR.alloc(NJ * DT)
            wj_c = AR.alloc(NJ * DT)
            wjl = [wj_a, wj_c]
            wtmp = AR.alloc(DT)
            coefS = AR.alloc(2 * 12 * NJ, F32)
            coefO = AR.alloc(2 * 12 * NJ, F32)
            creps = [(AR.alloc(128), AR.alloc(128)) for _ in range(2)]
            wts = [AR.alloc(128) for _ in range(4)]
            cmat = AR.alloc(NJ * 128)
            xm_a = AR.alloc(128)
            xm_b = AR.alloc(128)
            xm_t = AR.alloc(128)
            bvec = AR.alloc(NJ)
            bvec2 = AR.alloc(NJ)
            btmp = AR.alloc(2 * NJ, F32)

            def load_coefs(tag):
                for st in range(12):
                    P.dma("gpsimd", coefS[:, st * NJ:(st + 1) * NJ],
                          dr[f"f_self{tag}"][st], inc=sG)
                    P.dma("gpsimd", coefO[:, st * NJ:(st + 1) * NJ],
                          dr[f"f_other{tag}"][st], inc=sG)
                P.barrier(ALLSEMS)

            def build_cmats(rep_s, rep_o, stage_order):
                seq = [(jc, st) for jc in range(NJ) for st in stage_order]
                ld = {}
                dve_done = {-1: 0, -2: 0}

                def issue(q):
                    jc, st = seq[q]
                    pb = q % 2
                    off = (st * NJ + jc) * 128 * 128
                    P.dma("gpsimd", creps[pb][0][:],
                          bass.AP(dr[rep_s].tensor, off, [[128, 128], [1, 128]]),
                          waits=[(sV, dve_done[q - 2])], inc=sGp[pb])
                    P.dma("gpsimd", creps[pb][1][:],
                          bass.AP(dr[rep_o].tensor, off, [[128, 128], [1, 128]]),
                          inc=sGp[pb])
                    ld[q] = sGp[pb].n

                issue(0)
                issue(1)
                cur, nxt = xm_a, xm_b
                for q, (jc, st) in enumerate(seq):
                    pb = q % 2
                    crep_a, crep_b = creps[pb]
                    if st == stage_order[0]:
                        P.op("vector", lambda e: e.tensor_copy(xm_a[:], ident[:]),
                             waits=self_edge(sV) + [(sT, sT.n)], inc=sV)
                        cur, nxt = xm_a, xm_b
                    s_ = 1 << st
                    nb = 128 // (2 * s_)
                    shp = dict(a=nb, b=2, c=s_)
                    pview = bass.AP(cur.tensor, cur.offset + s_,
                                    [list(cur.ap[0])] + [[2 * s_, nb], [-s_, 2], [1, s_]])
                    P.op("vector", lambda e, pview=pview, shp=shp, crep_b=crep_b:
                         e.tensor_mul(
                             xm_t[:].rearrange("p (a b c) -> p a b c", **shp),
                             pview,
                             crep_b[:].rearrange("p (a b c) -> p a b c", **shp)),
                         waits=[(sGp[pb], ld[q])] + self_edge(sV), inc=sV)
                    P.op("vector", lambda e, cur=cur, nxt=nxt, crep_a=crep_a:
                         e.tensor_mul(nxt[:], cur[:], crep_a[:]),
                         waits=self_edge(sV), inc=sV)
                    P.op("vector", lambda e, nxt=nxt: e.tensor_add(nxt[:], nxt[:], xm_t[:]),
                         waits=self_edge(sV), inc=sV)
                    cur, nxt = nxt, cur
                    if st == stage_order[-1]:
                        P.op("vector", lambda e, jc=jc, cur=cur: e.tensor_copy(
                            cmat[:, jc * 128:(jc + 1) * 128], cur[:]),
                            waits=self_edge(sV), inc=sV)
                    dve_done[q] = sV.n
                    if q + 2 < len(seq):
                        issue(q + 2)
                P.barrier(ALLSEMS)

            def high_stages(cur, nxt, order, DTW, first_waits=()):
                for sti, st in enumerate(order):
                    cstride = (1 << st) // 128
                    for jc in range(NJ):
                        pc = jc ^ cstride
                        fw = list(first_waits) if (sti == 0 and jc == 0) else []
                        P.op("vector", lambda e, jc=jc, pc=pc, st=st, cur=cur: e.tensor_scalar(
                            wtmp[:, 0:DTW], cur[:, pc * DTW:(pc + 1) * DTW],
                            coefO[:, st * NJ + jc:st * NJ + jc + 1], None, ALU.mult),
                            waits=fw + self_edge(sV), inc=sV)
                        P.op("vector", lambda e, jc=jc, st=st, cur=cur, nxt=nxt:
                             e.scalar_tensor_tensor(
                                 nxt[:, jc * DTW:(jc + 1) * DTW],
                                 cur[:, jc * DTW:(jc + 1) * DTW],
                                 coefS[:, st * NJ + jc:st * NJ + jc + 1],
                                 wtmp[:, 0:DTW], ALU.mult, ALU.add),
                             waits=self_edge(sV), inc=sV)
                    cur, nxt = nxt, cur
                return cur, nxt

            def low_stages(cur, nxt, DTW, first_waits=()):
                cv = sV.n
                for jc in range(NJ):
                    bi = jc % 2
                    bank = pbank[bi]
                    fw = list(first_waits) if jc == 0 else []
                    P.op("tensor", lambda e, jc=jc, cur=cur, bank=bank: e.matmul(
                        bank[:, 0:DTW],
                        cmat[:, jc * 128:(jc + 1) * 128],
                        cur[:, jc * DTW:(jc + 1) * DTW], start=True, stop=True),
                        waits=[(sV, cv)] + fw + bwait(bi), inc=sT)
                    P.op("vector", lambda e, jc=jc, nxt=nxt, bank=bank: e.tensor_copy(
                        nxt[:, jc * DTW:(jc + 1) * DTW],
                        bank[:, 0:DTW]),
                        waits=[(sT, sT.n)] + self_edge(sV), inc=sV)
                    bfree(bi, sV)
                return nxt, cur

            def transform(w_name, high_first, dst_hbm, do_transpose):
                NDT = D // DT
                lds = {}
                rdv = {-1: (0, 0, 0), -2: (0, 0, 0)}
                slast = [sSs[i].n for i in range(4)]

                def issue_w(dt):
                    pb = dt % 2
                    v_, t_, o_ = rdv[dt - 2]
                    w = [(sV, v_), (sT, t_)]
                    if not do_transpose and dt >= 2:
                        w.append((sSs[pb], o_))
                    P.dma("gpsimd", wjl[pb][:],
                          bass.AP(dr[w_name].tensor, dt * DT,
                                  [[D, 128], [128 * D, NJ], [1, DT]]),
                          waits=w, inc=sGp[pb])
                    lds[dt] = sGp[pb].n

                issue_w(0)
                issue_w(1)
                for dt in range(NDT):
                    pb = dt % 2
                    fw = [(sGp[pb], lds[dt]), (sT, sT.n)]
                    cur, nxt = wjl[pb], wj_b
                    if high_first:
                        cur, nxt = high_stages(cur, nxt, (11, 10, 9, 8, 7), DT,
                                               first_waits=fw)
                        cur, nxt = low_stages(cur, nxt, DT)
                    else:
                        cur, nxt = low_stages(cur, nxt, DT, first_waits=fw)
                        cur, nxt = high_stages(cur, nxt, (7, 8, 9, 10, 11), DT)
                    if do_transpose:
                        vfirst = sV.n
                        for idx, (jc, ds) in enumerate(
                                (jc, ds) for jc in range(NJ) for ds in range(DT // 128)):
                            bi = 4 + (idx % 2)
                            si = idx % 4
                            bank = pbank[bi]
                            pview = bank[:, 0:64].bitcast(BF16)
                            P.op("tensor", lambda e, jc=jc, ds=ds, cur=cur, pview=pview:
                                 e.transpose(
                                     pview,
                                     cur[:, jc * DT + ds * 128: jc * DT + ds * 128 + 128],
                                     ident[:]),
                                 waits=[(sV, vfirst)] + bwait(bi), inc=sT)
                            dchunk = (dt * DT + ds * 128) // 128
                            P.op("vector", lambda e, si=si, pview=pview:
                                 e.tensor_copy(wts[si][:], pview),
                                 waits=[(sT, sT.n), (sSs[si], slast[si])]
                                 + self_edge(sV), inc=sV)
                            bfree(bi, sV)
                            P.dma("sync",
                                  bass.AP(dst_hbm.tensor,
                                          dchunk * DH + jc * 128,
                                          [[ND * DH, 128], [1, 128]]),
                                  wts[si][:], waits=[(sV, sV.n)], inc=sSs[si])
                            slast[si] = sSs[si].n
                    else:
                        for jc in range(NJ):
                            P.dma("sync",
                                  bass.AP(dst_hbm.tensor,
                                          jc * D + dt * DT,
                                          [[NJ * D, 128], [1, DT]]),
                                  cur[:, jc * DT:(jc + 1) * DT],
                                  waits=[(sV, sV.n), (sT, sT.n)], inc=sSs[pb])
                    rdv[dt] = (sV.n, sT.n, sSs[pb].n)
                    if dt + 2 < NDT:
                        issue_w(dt + 2)
                P.barrier(ALLSEMS)

            load_coefs("U")
            build_cmats("f_repSU", "f_repOU", (0, 1, 2, 3, 4, 5, 6))
            transform("f_WexpT", False, wexp_h, True)
            P.dma("gpsimd", bvec[:], dr["f_bexp"][:], inc=sG)
            P.dma("gpsimd", fb1[:], dr["f_upbias"][:], inc=sG)
            P.dma("gpsimd", fbo[:], dr["f_bcon"][:], inc=sG)
            P.dma("gpsimd", btmp[:, 0:ND], dr["f_dbias"][:], inc=sG)
            P.barrier(ALLSEMS)
            for jc in range(NJ):
                bank = pbank[jc % 2]
                P.op("tensor", lambda e, jc=jc, bank=bank: e.matmul(
                    bank[:, 0:1], cmat[:, jc * 128:(jc + 1) * 128],
                    bvec[:, jc:jc + 1], start=True, stop=True),
                    waits=[(sV, sV.n)], inc=sT)
                P.op("vector", lambda e, jc=jc, bank=bank: e.tensor_copy(
                    bvec2[:, jc:jc + 1], bank[:, 0:1]), waits=[(sT, sT.n)], inc=sV)
            cur, nxt = bvec2, bvec
            for st in (7, 8, 9, 10, 11):
                cstride = (1 << st) // 128
                for jc in range(NJ):
                    pc = jc ^ cstride
                    P.op("vector", lambda e, jc=jc, pc=pc, st=st, cur=cur: e.tensor_scalar(
                        wtmp[:, 0:1], cur[:, pc:pc + 1],
                        coefO[:, st * NJ + jc:st * NJ + jc + 1], None, ALU.mult),
                        waits=self_edge(sV), inc=sV)
                    P.op("vector", lambda e, jc=jc, st=st, cur=cur, nxt=nxt:
                         e.scalar_tensor_tensor(
                             nxt[:, jc:jc + 1], cur[:, jc:jc + 1],
                             coefS[:, st * NJ + jc:st * NJ + jc + 1],
                             wtmp[:, 0:1], ALU.mult, ALU.add),
                         waits=self_edge(sV), inc=sV)
                cur, nxt = nxt, cur
            P.op("vector", lambda e, cur=cur: e.tensor_add(fb1[:], fb1[:], cur[:]),
                 waits=self_edge(sV), inc=sV)
            P.op("vector", lambda e: e.tensor_add(fbo[:], fbo[:], btmp[:, 0:ND]),
                 waits=self_edge(sV), inc=sV)
            P.barrier(ALLSEMS)
            load_coefs("D")
            build_cmats("f_repSD", "f_repOD", (6, 5, 4, 3, 2, 1, 0))
            transform("f_Wcon", True, wcon_h, False)
            AR.pop()

        def ln_tile(src_t, src_off, it, xt8, xsq8, xn8, rows, bcr, bcmr, xld_waits=None,
                    xsem=None, xsq_waits=()):
            t0 = it * TT
            if xsem is None:
                xsem = sG
            if xld_waits is None:
                xld_waits = [(sV, sV.n), (sT, sT.n), (sA, sA.n)]
            P.dma("gpsimd", xt8[:],
                  bass.AP(src_t, src_off + t0, [[S, 128], [128 * S, ND], [1, TT]]),
                  waits=xld_waits, inc=xsem)
            gg = xsem.n
            for dc in range(ND):
                P.op("scalar", lambda e, dc=dc: e.activation(
                    xsq8[:, dc * TT:(dc + 1) * TT], xt8[:, dc * TT:(dc + 1) * TT], AF.Square),
                    waits=(([(xsem, gg)] + list(xsq_waits)) if dc == 0 else ()), inc=sA)
            aa = sA.n
            vfree = sV.n
            for which, buf in ((0, xt8), (1, xsq8)):
                for dc in range(ND):
                    P.op("tensor", lambda e, which=which, dc=dc, buf=buf: e.matmul(
                        pbank[which][0:1, 0:TT], ones_colb[:],
                        buf[:, dc * TT:(dc + 1) * TT],
                        start=(dc == 0), stop=(dc == ND - 1)),
                        waits=(([(xsem, gg), (sV, vfree)] if which == 0 else [(sA, aa)])
                               if dc == 0 else ()),
                        inc=(sT if dc == ND - 1 else None))
            tsum = sT.n
            rowm, rowv, rowr, rowmr = rows
            P.op("vector", lambda e: e.tensor_scalar(
                rowm[:], pbank[0][0:1, 0:TT], 1.0 / D, None, ALU.mult),
                waits=[(sT, tsum)] + self_edge(sV), inc=sV)
            P.op("vector", lambda e: e.tensor_scalar(
                rowv[:], pbank[1][0:1, 0:TT], 1.0 / D, None, ALU.mult),
                waits=self_edge(sV), inc=sV)
            P.op("scalar", lambda e: e.activation(bcr[0:1, 0:TT], rowm[:], AF.Square),
                 waits=[(sV, sV.n)], inc=sA)
            P.op("vector", lambda e: e.tensor_sub(rowv[:], rowv[:], bcr[0:1, 0:TT]),
                 waits=[(sA, sA.n)] + self_edge(sV), inc=sV)
            P.op("scalar", lambda e: e.activation(rowr[:], rowv[:], AF.Ln, bias=1e-5),
                 waits=[(sV, sV.n)], inc=sA)
            P.op("scalar", lambda e: e.activation(rowr[:], rowr[:], AF.Exp, scale=-0.5),
                 waits=self_edge(sA), inc=sA)
            P.op("vector", lambda e: e.tensor_mul(rowmr[:], rowm[:], rowr[:]),
                 waits=[(sA, sA.n)] + self_edge(sV), inc=sV)
            stv, sta = sV.n, sA.n
            P.op("tensor", lambda e: e.matmul(
                pbank[2][:, 0:TT], ones_rowf, rowr[:], start=True, stop=True),
                waits=[(sA, sta), (sV, stv)], inc=sT)
            P.op("tensor", lambda e: e.matmul(
                pbank[3][:, 0:TT], ones_rowf, rowmr[:], start=True, stop=True),
                inc=sT)
            tb = sT.n
            P.op("vector", lambda e: e.tensor_copy(bcr[:], pbank[2][:, 0:TT]),
                 waits=[(sT, tb)] + self_edge(sV), inc=sV)
            P.op("vector", lambda e: e.tensor_copy(bcmr[:], pbank[3][:, 0:TT]),
                 waits=self_edge(sV), inc=sV)
            for dc in range(ND):
                P.op("vector", lambda e, dc=dc: e.tensor_mul(
                    xn8[:, dc * TT:(dc + 1) * TT], xt8[:, dc * TT:(dc + 1) * TT], bcr[:]),
                    waits=self_edge(sV), inc=sV)
                P.op("vector", lambda e, dc=dc: e.tensor_sub(
                    xn8[:, dc * TT:(dc + 1) * TT], xn8[:, dc * TT:(dc + 1) * TT], bcmr[:]),
                    waits=self_edge(sV), inc=sV)
            return tsum

        def ffn_call(b, src_ap, dst_ap, final):
            AR.push()
            wexp = AR.alloc(ND * DH)
            wcon = AR.alloc(NJ * D)
            g_t = AR.alloc(NJ * TT)
            xsq_shared = AR.alloc(ND * TT)
            bcr_shared = AR.alloc(TT)
            bcmr_shared = AR.alloc(TT)
            rowbuf_shared = AR.alloc(2 * 2 * TT, F32)
            rows_shared = (rowbuf_shared[0:1, 0:TT], rowbuf_shared[0:1, TT:2 * TT],
                           rowbuf_shared[0:1, TT:2 * TT], rowbuf_shared[0:1, 0:TT])
            ctxs = []
            for _ in range(2):
                ctxs.append(dict(
                    xt8=AR.alloc(ND * TT),
                    xsq8=xsq_shared,
                    xn8=AR.alloc(ND * TT),
                    bcr=bcr_shared,
                    bcmr=bcmr_shared,
                    rowbuf=rowbuf_shared,
                    rows=rows_shared,
                ))

            P.dma("sync", wexp[:], wexp_h[:], waits=[(sT, sT.n)], inc=sQ)
            P.dma("sync", wcon[:], wcon_h[:], inc=sQ)
            P.barrier(ALLSEMS)

            res_done = {-1: 0, -2: 0}
            h_done = {-1: 0, -2: 0}
            od = {-1: 0, -2: 0}
            stats_done = {-1: 0}
            vln = {}

            def emit_ln(it):
                cx = ctxs[it % 2]
                w = [(sV, res_done[it - 2]), (sT, h_done[it - 2]), (sO, od[it - 2])]
                xw = [(sT, stats_done[it - 1])]
                stats_done[it] = ln_tile(
                    src_ap.tensor, src_ap.offset, it, cx["xt8"], cx["xsq8"],
                    cx["xn8"], cx["rows"], cx["bcr"], cx["bcmr"], xld_waits=w,
                    xsem=sNp[it % 2], xsq_waits=xw)
                vln[it] = sV.n

            emit_ln(0)
            for it in range(NTT):
                t0 = it * TT
                cx = ctxs[it % 2]
                xn8 = cx["xn8"]
                for jc in range(NJ):
                    bi = 4 + (jc % 2)
                    bank = pbank[bi]
                    for dc in range(ND):
                        P.op("tensor", lambda e, jc=jc, dc=dc, bank=bank, xn8=xn8: e.matmul(
                            bank[:, 0:TT],
                            wexp[:, dc * DH + jc * 128: dc * DH + (jc + 1) * 128],
                            xn8[:, dc * TT:(dc + 1) * TT],
                            start=(dc == 0), stop=(dc == ND - 1)),
                            waits=(([(sV, vln[it])] + bwait(bi)) if dc == 0 else ()),
                            inc=(sT if dc == ND - 1 else None))
                    P.op("scalar", lambda e, jc=jc, bank=bank: e.activation(
                        g_t[:, jc * TT:(jc + 1) * TT], bank[:, 0:TT], AF.Gelu,
                        bias=fb1[:, jc:jc + 1]),
                        waits=[(sT, sT.n)], inc=sA)
                    bfree(bi, sA)
                h_done[it] = sT.n
                aa_g = sA.n
                if it + 1 < NTT:
                    emit_ln(it + 1)
                for oc in range(ND):
                    bi = 6 + (oc % 2)
                    bank = pbank[bi]
                    for hc in range(NJ):
                        P.op("tensor", lambda e, oc=oc, hc=hc, bank=bank: e.matmul(
                            bank[:, 0:TT],
                            wcon[:, hc * D + oc * 128: hc * D + (oc + 1) * 128],
                            g_t[:, hc * TT:(hc + 1) * TT],
                            start=(hc == 0), stop=(hc == NJ - 1)),
                            waits=(([(sA, aa_g)] + bwait(bi)) if hc == 0 else ()),
                            inc=(sT if hc == NJ - 1 else None))
                    tdone = sT.n
                    if final:
                        stage = cx["xn8"][:, 0:2 * TT].bitcast(F32)[:, 0:TT]
                    else:
                        stage = cx["xn8"][:, oc * TT:(oc + 1) * TT]
                    P.op("vector", lambda e, oc=oc, bank=bank, stage=stage, cx=cx, t0=t0:
                         e.scalar_tensor_tensor(
                             stage,
                             bank[:, 0:TT], fbo[:, oc:oc + 1],
                             cx["xt8"][:, oc * TT:(oc + 1) * TT],
                             ALU.add, ALU.add),
                         waits=[(sT, tdone), (sO, sO.n)] + self_edge(sV), inc=sV)
                    bfree(bi, sV)
                    P.dma("sync",
                          bass.AP(dst_ap.tensor, dst_ap.offset + oc * 128 * S + t0,
                                  [[S, 128], [1, TT]]),
                          stage,
                          waits=[(sV, sV.n)], inc=sO)
                res_done[it] = sV.n
                od[it] = sO.n
            P.barrier(ALLSEMS)
            AR.pop()

        def identity_call(b, src_ap, dst_ap, final):
            AR.push()
            buf = AR.alloc(ND * TT, F32 if final else BF16)
            for it in range(NTT):
                t0 = it * TT
                P.dma("gpsimd", buf[:],
                      bass.AP(src_ap.tensor, src_ap.offset + t0,
                              [[S, 128], [128 * S, ND], [1, TT]]),
                      waits=[(sO, sO.n)], inc=sG)
                for dc in range(ND):
                    P.dma("sync",
                          bass.AP(dst_ap.tensor, dst_ap.offset + dc * 128 * S + t0,
                                  [[S, 128], [1, TT]]),
                          buf[:, dc * TT:(dc + 1) * TT],
                          waits=[(sG, sG.n)], inc=sO)
            P.barrier(ALLSEMS)
            AR.pop()


        def attn_prep():
            AR.push()
            stage = AR.alloc(3 * D)
            for dc in range(ND):
                P.dma("gpsimd", stage[:],
                      bass.AP(dr["a_WqkvT"].tensor, dc * 128 * 3 * D,
                              [[3 * D, 128], [1, 3 * D]]),
                      waits=[(sQ, sQ.n)], inc=sG2)
                P.dma("sync",
                      bass.AP(wqkv_h.tensor, dc * 3 * D, [[ND * 3 * D, 128], [1, 3 * D]]),
                      stage[:], waits=[(sG2, sG2.n)], inc=sQ)
            for dc in range(ND):
                P.dma("gpsimd", stage[:, 0:D],
                      bass.AP(dr["a_WoT"].tensor, dc * 128 * D, [[D, 128], [1, D]]),
                      waits=[(sQ, sQ.n)], inc=sG2)
                P.dma("sync",
                      bass.AP(woT_h.tensor, dc * D, [[ND * D, 128], [1, D]]),
                      stage[:, 0:D], waits=[(sG2, sG2.n)], inc=sQ)
            P.barrier(ALLSEMS)
            AR.pop()

        def attn_a_call(b, src_ap, dst_ap, final):
            NQ = S // TT
            AR.push()
            ident32 = AR.alloc(256, F32)
            bvbc = AR.alloc(D)
            wq_col = AR.alloc(2, F32)[:, 0:1]
            wk_col = AR.alloc(2, F32)[:, 0:1]
            bq_c = AR.alloc(2 * ND, F32)
            bk_c = AR.alloc(2 * ND, F32)
            rsum_c = AR.alloc(4)[:, 0:2]
            sel2_c = AR.alloc(2 * 128, F32)
            bo_c = AR.alloc(2 * ND, F32)
            P.dma("sync", ident32[:], dr["c_ident"][:], waits=[(sT, sT.n)], inc=sQ)
            P.dma("gpsimd", wq_col, dr["a_wq"][:], waits=[(sV, sV.n)], inc=sG)
            P.dma("gpsimd", wk_col, dr["a_wk"][:], inc=sG)
            P.dma("gpsimd", bq_c[:], dr["a_bq"][:], inc=sG)
            P.dma("gpsimd", bk_c[:], dr["a_bk"][:], inc=sG)
            P.dma("gpsimd", rsum_c, dr["c_rsum"][:], inc=sG)
            P.dma("gpsimd", sel2_c[0:2, :], dr["c_sel2"][:], inc=sG)
            P.dma("gpsimd", bo_c[:], dr["a_bo"][:], inc=sG)
            P.dma("gpsimd", bvbc[0:1, 0:D], dr["a_bvrow"][:], inc=sG)
            gg0 = sG.n
            P.op("tensor", lambda e: e.matmul(
                pbank[0][:, 0:512],
                ones_rowb, bvbc[0:1, 0:512], start=True, stop=True),
                waits=[(sG, gg0)] + bwait(0), inc=sT)
            P.op("tensor", lambda e: e.matmul(
                pbank[1][:, 0:512],
                ones_rowb, bvbc[0:1, 512:1024], start=True, stop=True),
                waits=bwait(1), inc=sT)
            tb0 = sT.n
            P.op("vector", lambda e: e.tensor_copy(bvbc[:, 0:512], pbank[0][:, 0:512]),
                 waits=[(sT, tb0)] + self_edge(sV), inc=sV)
            P.op("vector", lambda e: e.tensor_copy(bvbc[:, 512:1024], pbank[1][:, 0:512]),
                 waits=self_edge(sV), inc=sV)
            bfree(0, sV)
            bfree(1, sV)
            P.barrier(ALLSEMS)
            qf = AR.alloc(ND * S)
            kf = AR.alloc(ND * S)
            vt = AR.alloc(NT * D)
            zf = AR.alloc(ND * S)
            UA = 3 * D * ND
            ua = AR.alloc(UA)
            xt8 = AR.alloc(ND * TT)
            xsq8 = zf[:, 0:ND * TT]
            xn8 = AR.alloc(ND * TT)
            bcr = AR.alloc(TT)
            bcmr = AR.alloc(TT)
            rowbuf = AR.alloc(2 * 2 * TT, F32)
            rows = (rowbuf[0:1, 0:TT], rowbuf[0:1, TT:2 * TT],
                    rowbuf[0:1, TT:2 * TT], rowbuf[0:1, 0:TT])
            xo32 = AR.alloc(2 * TT, F32)
            qsq = bcr
            rstd2 = xo32
            negm = AR.alloc(2 * 2 * NT, F32)

            wqkv = ua
            kaug = ua[:, 0:S]
            qaug = ua[:, S:2 * S]
            eb = ua[:, 2 * S:2 * S + NT * TT]
            vaug = ua[:, 2 * S + NT * TT:2 * S + NT * TT + NT * 72]
            wo = ua[:, 2 * S + NT * TT + NT * 72: 2 * S + NT * TT + NT * 72 + ND * D]
            mrowbuf = AR.alloc(128)

            P.dma("sync", wqkv[:], wqkv_h[:], inc=sQ)
            P.barrier(ALLSEMS)

            for it in range(NTT):
                t0 = it * TT
                ln_tile(src_ap.tensor, src_ap.offset, it, xt8, xsq8, xn8, rows, bcr, bcmr)
                vln = sV.n
                for which, dstb, bias in ((0, qf, bq_c), (1, kf, bk_c)):
                    for ec in range(ND):
                        bi = 4 + (ec % 2)
                        bank = pbank[bi]
                        for dc in range(ND):
                            P.op("tensor", lambda e, which=which, ec=ec, dc=dc, bank=bank:
                                 e.matmul(
                                     bank[:, 0:TT],
                                     wqkv[:, dc * 3 * D + which * D + ec * 128:
                                          dc * 3 * D + which * D + (ec + 1) * 128],
                                     xn8[:, dc * TT:(dc + 1) * TT],
                                     start=(dc == 0), stop=(dc == ND - 1)),
                                 waits=(([(sV, vln)] + bwait(bi)) if dc == 0 else ()),
                                 inc=(sT if dc == ND - 1 else None))
                        P.op("vector", lambda e, ec=ec, dstb=dstb, bias=bias, bank=bank, t0=t0:
                             e.tensor_scalar(
                                 dstb[:, ec * S + t0: ec * S + t0 + TT],
                                 bank[:, 0:TT], bias[:, ec:ec + 1], None, ALU.add),
                             waits=[(sT, sT.n)], inc=sV)
                        bfree(bi, sV)
                for tc in range(TT // 128):
                    tch = it * (TT // 128) + tc
                    for half in range(2):
                        bi = 6 + half
                        bank = pbank[bi]
                        for dc in range(ND):
                            P.op("tensor", lambda e, tc=tc, half=half, dc=dc, bank=bank, t0=t0:
                                 e.matmul(
                                     bank[:, 0:512],
                                     xn8[:, dc * TT + tc * 128: dc * TT + (tc + 1) * 128],
                                     wqkv[:, dc * 3 * D + 2 * D + half * 512:
                                          dc * 3 * D + 2 * D + (half + 1) * 512],
                                     start=(dc == 0), stop=(dc == ND - 1)),
                                 waits=(bwait(bi) if dc == 0 else ()),
                                 inc=(sT if dc == ND - 1 else None))
                        P.op("vector", lambda e, tch=tch, half=half, bank=bank:
                             e.scalar_tensor_tensor(
                                 vt[:, tch * D + half * 512: tch * D + (half + 1) * 512],
                                 bank[:, 0:512], 1.0,
                                 bvbc[:, half * 512:(half + 1) * 512],
                                 ALU.mult, ALU.add),
                             waits=[(sT, sT.n)], inc=sV)
                        bfree(bi, sV)
            P.barrier(ALLSEMS)

            for which, buf, wcol in ((0, qf, wq_col), (1, kf, wk_col)):
                for ec in range(ND):
                    for iq in range(NQ):
                        t0 = iq * TT
                        P.op("scalar", lambda e, buf=buf, ec=ec, t0=t0: e.activation(
                            qsq[:], buf[:, ec * S + t0: ec * S + t0 + TT], AF.Square),
                            waits=[(sV, sV.n)] + self_edge(sA), inc=sA)
                        bi = 0
                        P.op("tensor", lambda e, bank=pbank[bi]: e.matmul(
                            bank[0:2, 0:TT], rsum_c, qsq[:], start=True, stop=True),
                            waits=[(sA, sA.n)] + bwait(bi), inc=sT)
                        P.op("scalar", lambda e, bank=pbank[bi]: e.activation(
                            rstd2[0:2, :], bank[0:2, 0:TT], AF.Ln,
                            scale=1.0 / 64.0, bias=1e-6),
                            waits=[(sT, sT.n)] + self_edge(sA), inc=sA)
                        bfree(bi, sA)
                        P.op("scalar", lambda e: e.activation(
                            rstd2[0:2, :], rstd2[0:2, :], AF.Exp, scale=-0.5),
                            waits=self_edge(sA), inc=sA)
                        bi2 = 1
                        P.op("tensor", lambda e, bank=pbank[bi2]: e.matmul(
                            bank[:, 0:TT], sel2_c[0:2, :], rstd2[0:2, :],
                            start=True, stop=True),
                            waits=[(sA, sA.n)] + bwait(bi2), inc=sT)
                        P.op("vector", lambda e, buf=buf, ec=ec, t0=t0, wcol=wcol,
                             bank=pbank[bi2]: e.scalar_tensor_tensor(
                                 buf[:, ec * S + t0: ec * S + t0 + TT],
                                 buf[:, ec * S + t0: ec * S + t0 + TT],
                                 wcol, bank[:, 0:TT], ALU.mult, ALU.mult),
                             waits=[(sT, sT.n)] + self_edge(sV), inc=sV)
                        bfree(bi2, sV)
            P.barrier(ALLSEMS)

            for tc in range(NT):
                P.op("vector", lambda e, tc=tc: e.memset(
                    vaug[:, tc * 72 + 64: tc * 72 + 65], 1.0), inc=sV)
            P.barrier(ALLSEMS)

            qsqb = xt8
            qln = xn8[:, 0:2 * S].bitcast(F32)[0:1, 0:S]
            kq4 = xo32[0:1, 0:NQ]
            km2 = xo32[0:1, NQ:NQ + 1]
            klnb = xo32[0:1, NQ + 1:NQ + 2]
            for hp in range(8):
                ec = hp
                for half in range(2):
                    h = 2 * hp + half
                    base = half * 64
                    P.op("vector", lambda e, base=base, ec=ec: e.tensor_mul(
                        qsqb[0:64, 0:S],
                        kf[base:base + 64, ec * S:(ec + 1) * S],
                        kf[base:base + 64, ec * S:(ec + 1) * S]),
                        waits=[(sT, sT.n)] + self_edge(sV), inc=sV)
                    vsq = sV.n
                    for iq in range(NQ):
                        bi = iq % 2
                        P.op("tensor", lambda e, iq=iq, bi=bi: e.matmul(
                            pbank[bi][0:1, 0:TT], rsum_c[:, 0:1],
                            qsqb[:, iq * TT:(iq + 1) * TT],
                            start=True, stop=True),
                            waits=[(sV, vsq)] + bwait(bi), inc=sT)
                        P.op("vector", lambda e, iq=iq, bi=bi: e.reduce_max(
                            kq4[:, iq:iq + 1], pbank[bi][0:1, 0:TT], axis=AX.X),
                            waits=[(sT, sT.n)] + self_edge(sV), inc=sV)
                        bfree(bi, sV)
                    P.op("vector", lambda e: e.reduce_max(km2, kq4, axis=AX.X),
                         waits=[(sA, sA.n)] + self_edge(sV), inc=sV)
                    vkm = sV.n
                    P.op("scalar", lambda e: e.activation(klnb, km2, AF.Ln),
                         waits=[(sV, vkm)] + self_edge(sA), inc=sA)
                    P.op("scalar", lambda e: e.activation(klnb, klnb, AF.Copy,
                                                          scale=0.5, bias=-0.4),
                         waits=self_edge(sA), inc=sA)
                    P.op("vector", lambda e, base=base, ec=ec: e.tensor_mul(
                        qsqb[0:64, 0:S],
                        qf[base:base + 64, ec * S:(ec + 1) * S],
                        qf[base:base + 64, ec * S:(ec + 1) * S]),
                        waits=[(sT, sT.n)] + self_edge(sV), inc=sV)
                    vsq2 = sV.n
                    for iq in range(NQ):
                        bi = iq % 2
                        P.op("tensor", lambda e, iq=iq, bi=bi: e.matmul(
                            pbank[bi][0:1, 0:TT], rsum_c[:, 0:1],
                            qsqb[:, iq * TT:(iq + 1) * TT],
                            start=True, stop=True),
                            waits=[(sV, vsq2)] + bwait(bi), inc=sT)
                        P.op("scalar", lambda e, iq=iq, bi=bi: e.activation(
                            qln[:, iq * TT:(iq + 1) * TT], pbank[bi][0:1, 0:TT], AF.Ln),
                            waits=[(sT, sT.n)] + self_edge(sA), inc=sA)
                        bfree(bi, sA)
                    vv = sV.n
                    P.op("vector", lambda e, base=base, ec=ec: e.tensor_copy(
                        kaug[0:64, :], kf[base:base + 64, ec * S:(ec + 1) * S]),
                        waits=[(sV, vv)] + [(sT, sT.n)], inc=sV)
                    P.op("vector", lambda e, base=base, ec=ec: e.tensor_copy(
                        qaug[0:64, :], qf[base:base + 64, ec * S:(ec + 1) * S]),
                        waits=self_edge(sV), inc=sV)
                    P.op("vector", lambda e: e.memset(kaug[64:65, :], -1.0),
                         waits=self_edge(sV), inc=sV)
                    P.op("scalar", lambda e: e.activation(
                        qaug[64:65, 0:S], qln, AF.Exp, scale=0.5, bias=klnb),
                        waits=[(sT, sT.n)] + self_edge(sA), inc=sA)
                    for tc in range(NT):
                        P.op("vector", lambda e, tc=tc, h=h: e.tensor_copy(
                            vaug[:, tc * 72: tc * 72 + 64],
                            vt[:, tc * D + h * 64: tc * D + h * 64 + 64]),
                            waits=self_edge(sV), inc=sV)
                    vaugdone = sV.n
                    aaug = sA.n
                    for iq in range(NQ):
                        t0 = iq * TT
                        for tc in range(NT):
                            bi = (tc % 2)
                            P.op("tensor", lambda e, tc=tc, t0=t0, bi=bi: e.matmul(
                                pbank[bi][:, 0:TT],
                                kaug[0:65, tc * 128:(tc + 1) * 128],
                                qaug[0:65, t0:t0 + TT],
                                start=True, stop=True),
                                waits=[(sA, aaug), (sV, vaugdone)] + bwait(bi), inc=sT)
                            P.op("scalar", lambda e, tc=tc, bi=bi: e.activation(
                                eb[:, tc * TT:(tc + 1) * TT], pbank[bi][:, 0:TT], AF.Exp),
                                waits=[(sT, sT.n)] + self_edge(sA), inc=sA)
                            bfree(bi, sA)
                        aexp = sA.n
                        bi = 3
                        for tc in range(NT):
                            P.op("tensor", lambda e, tc=tc, bi=bi: e.matmul(
                                pbank[bi][0:65, 0:TT],
                                vaug[:, tc * 72: tc * 72 + 65],
                                eb[:, tc * TT:(tc + 1) * TT],
                                start=(tc == 0), stop=(tc == NT - 1)),
                                waits=(([(sA, aexp)] + bwait(bi)) if tc == 0 else ()),
                                inc=(sT if tc == NT - 1 else None))
                        tav = sT.n
                        P.op("vector", lambda e, bi=bi: e.reciprocal(
                            rstd2[0:1, :], pbank[bi][64:65, 0:TT]),
                            waits=[(sT, tav)] + self_edge(sV), inc=sV)
                        P.op("vector", lambda e, bi=bi: e.tensor_copy(
                            qsq[0:64, 0:TT], pbank[bi][0:64, 0:TT]),
                            waits=self_edge(sV), inc=sV)
                        bfree(bi, sV)
                        bi2 = 2
                        P.op("tensor", lambda e, bi2=bi2: e.matmul(
                            pbank[bi2][0:64, 0:TT], ones_rowf[:, 0:64], rstd2[0:1, :],
                            start=True, stop=True),
                            waits=[(sV, sV.n)] + bwait(bi2), inc=sT)
                        P.op("vector", lambda e, h=h, t0=t0, base=base, ec=ec, bi2=bi2:
                             e.tensor_tensor(
                                 zf[base:base + 64, ec * S + t0: ec * S + t0 + TT],
                                 qsq[0:64, 0:TT], pbank[bi2][0:64, 0:TT], ALU.mult),
                             waits=[(sT, sT.n)] + self_edge(sV), inc=sV)
                        bfree(bi2, sV)
            P.barrier(ALLSEMS)

            P.dma("sync", wo[:], woT_h[:], inc=sQ)
            P.barrier(ALLSEMS)

            for iq in range(NQ):
                t0 = iq * TT
                P.dma("gpsimd", xt8[:],
                      bass.AP(src_ap.tensor, src_ap.offset + t0,
                              [[S, 128], [128 * S, ND], [1, TT]]),
                      waits=[(sV, sV.n), (sO, sO.n)], inc=sG)
                gx = sG.n
                for oc in range(ND):
                    bi = 4 + (oc % 2)
                    bank = pbank[bi]
                    for ecc in range(ND):
                        P.op("tensor", lambda e, oc=oc, ecc=ecc, bank=bank, t0=t0: e.matmul(
                            bank[:, 0:TT],
                            wo[:, ecc * D + oc * 128: ecc * D + (oc + 1) * 128],
                            zf[:, ecc * S + t0: ecc * S + t0 + TT],
                            start=(ecc == 0), stop=(ecc == ND - 1)),
                            waits=(bwait(bi) if ecc == 0 else ()),
                            inc=(sT if ecc == ND - 1 else None))
                    tdone = sT.n
                    stg = xo32 if final else xn8[:, oc * TT:(oc + 1) * TT]
                    P.op("vector", lambda e, oc=oc, bank=bank, stg=stg:
                         e.scalar_tensor_tensor(
                             stg[:, 0:TT] if stg is xo32 else stg,
                             bank[:, 0:TT], bo_c[:, oc:oc + 1],
                             xt8[:, oc * TT:(oc + 1) * TT],
                             ALU.add, ALU.add),
                         waits=[(sT, tdone), (sG, gx), (sO, sO.n)] + self_edge(sV), inc=sV)
                    bfree(bi, sV)
                    P.dma("sync",
                          bass.AP(dst_ap.tensor, dst_ap.offset + oc * 128 * S + t0,
                                  [[S, 128], [1, TT]]),
                          stg[:, 0:TT] if stg is xo32 else stg,
                          waits=[(sV, sV.n)], inc=sO)
            P.barrier(ALLSEMS)
            AR.pop()


        def attn_b_prep():
            AR.push()
            stage = AR.alloc(3 * D)
            for dc in range(ND):
                P.dma("gpsimd", stage[:],
                      bass.AP(dr["b_WqkvT"].tensor, dc * 128 * 3 * D,
                              [[3 * D, 128], [1, 3 * D]]),
                      waits=[(sQ, sQ.n)], inc=sG2)
                P.dma("sync",
                      bass.AP(wqkvB_h.tensor, dc * 3 * D, [[ND * 3 * D, 128], [1, 3 * D]]),
                      stage[:], waits=[(sG2, sG2.n)], inc=sQ)
            for dc in range(ND):
                P.dma("gpsimd", stage[:, 0:D],
                      bass.AP(dr["b_WoT"].tensor, dc * 128 * D, [[D, 128], [1, D]]),
                      waits=[(sQ, sQ.n)], inc=sG2)
                P.dma("sync",
                      bass.AP(woB_h.tensor, dc * D, [[ND * D, 128], [1, D]]),
                      stage[:, 0:D], waits=[(sG2, sG2.n)], inc=sQ)
            for (srcn, dstt, nrow, ncol) in (("b_Fc", Fc_h, S, NF2), ("b_Fs", Fs_h, S, NF2),
                                             ("b_Cr", Cr_h, NF2, S), ("b_Ci", Ci_h, NF2, S)):
                for rc in range(nrow // 128):
                    P.dma("gpsimd", stage[:, 0:ncol] if ncol <= 3 * D else None,
                          bass.AP(dr[srcn].tensor, rc * 128 * ncol, [[ncol, 128], [1, ncol]]),
                          waits=[(sQ, sQ.n)], inc=sG2)
                    P.dma("sync",
                          bass.AP(dstt.tensor, rc * 128 * ncol, [[ncol, 128], [1, ncol]]),
                          stage[:, 0:ncol], waits=[(sG2, sG2.n)], inc=sQ)
            P.barrier(ALLSEMS)
            AR.pop()
            AR.push()
            gbuf = AR.alloc(NT * D)
            fbs = [AR.alloc(2 * NT * 128) for _ in range(2)]
            gsts = [AR.alloc(2 * D) for _ in range(2)]
            kdr = AR.alloc(D)
            for tc in range(NT):
                P.dma("gpsimd", gbuf[:, tc * D:(tc + 1) * D],
                      bass.AP(dr["b_kernT"].tensor, tc * 128 * D, [[D, 128], [1, D]]),
                      inc=sG)
            P.dma("gpsimd", kdr[0:1, :], dr["b_kDrow"][:], inc=sG)
            P.barrier(ALLSEMS)
            P.op("vector", lambda e: e.tensor_add(gbuf[0:1, 0:D], gbuf[0:1, 0:D], kdr[0:1, :]),
                 inc=sV)
            P.barrier(ALLSEMS)

            def fwd_dft(rhsbuf, out_r, out_i, extra_cb=None):
                NFCp = NF2 // 128
                fld = {}
                tdn = {-1: 0, -2: 0}
                gslast = [sSs[0].n, sSs[1].n]

                def issue_f(fc):
                    pb = fc % 2
                    w = ([(sT, tdn[fc - 2])] if fc >= 2
                         else [(sV, sV.n), (sA, sA.n), (sT, sT.n)])
                    P.dma("sync", fbs[pb][:, 0:NT * 128],
                          bass.AP(Fc_h.tensor, fc * 128,
                                  [[NF2, 128], [128 * NF2, NT], [1, 128]]),
                          waits=w, inc=sFp[pb])
                    P.dma("sync", fbs[pb][:, NT * 128:2 * NT * 128],
                          bass.AP(Fs_h.tensor, fc * 128,
                                  [[NF2, 128], [128 * NF2, NT], [1, 128]]),
                          inc=sFp[pb])
                    fld[fc] = sFp[pb].n

                issue_f(0)
                issue_f(1)
                for fc in range(NFCp):
                    pb = fc % 2
                    for ri in range(2):
                        for half in range(2):
                            bi = 2 * ri + half
                            for tc in range(NT):
                                P.op("tensor", lambda e, ri=ri, half=half, tc=tc, bi=bi,
                                     pb=pb, rhsbuf=rhsbuf: e.matmul(
                                         pbank[bi][:, 0:512],
                                         fbs[pb][:, ri * NT * 128 + tc * 128:
                                                 ri * NT * 128 + (tc + 1) * 128],
                                         rhsbuf[:, tc * D + half * 512:
                                                tc * D + (half + 1) * 512],
                                         start=(tc == 0), stop=(tc == NT - 1)),
                                     waits=(([(sFp[pb], fld[fc])] + bwait(bi))
                                            if tc == 0 else ()),
                                     inc=(sT if tc == NT - 1 else None))
                    tfd = sT.n
                    tdn[fc] = tfd
                    if extra_cb is None:
                        gs = gsts[pb]
                        for ri, outt in ((0, out_r), (1, out_i)):
                            for half in range(2):
                                bi = 2 * ri + half
                                P.op("vector", lambda e, ri=ri, half=half, bi=bi, gs=gs:
                                     e.tensor_copy(gs[:, ri * D + half * 512:
                                                      ri * D + (half + 1) * 512],
                                                   pbank[bi][:, 0:512]),
                                     waits=[(sT, tfd), (sSs[pb], gslast[pb])]
                                     + self_edge(sV), inc=sV)
                                bfree(bi, sV)
                            P.dma("sync",
                                  bass.AP(outt.tensor, fc * 128 * D, [[D, 128], [1, D]]),
                                  gs[:, ri * D:(ri + 1) * D],
                                  waits=[(sV, sV.n)], inc=sSs[pb])
                        gslast[pb] = sSs[pb].n
                    else:
                        extra_cb(fc, tfd)
                    if fc + 2 < NFCp:
                        issue_f(fc + 2)

            fwd_dft(gbuf, Gr_h, Gi_h)
            P.barrier(ALLSEMS)
            AR.pop()

        def attn_b_call(src_of, dst_of, final):
            assert B == 4, "attn_b kv layout assumes B=4 per core"
            AR.push()
            wqkv = AR.alloc(ND * 3 * D)
            xsq8 = AR.alloc(ND * TT)
            bcr = AR.alloc(TT)
            bcmr = AR.alloc(TT)
            rowbuf = AR.alloc(2 * 2 * TT, F32)
            rows = (rowbuf[0:1, 0:TT], rowbuf[0:1, TT:2 * TT],
                    rowbuf[0:1, TT:2 * TT], rowbuf[0:1, 0:TT])
            cxs = [dict(xt8=AR.alloc(ND * TT), xn8=AR.alloc(ND * TT)) for _ in range(2)]
            bqB = AR.alloc(2 * ND, F32)
            bkbc = AR.alloc(D)
            bvbcB = AR.alloc(D)
            prsts = [AR.alloc(512) for _ in range(4)]
            rowtmp = AR.alloc(D)

            P.dma("sync", wqkv[:], wqkvB_h[:], waits=[(sT, sT.n)], inc=sQ)
            P.dma("gpsimd", bqB[:], dr["b_bq"][:], waits=[(sV, sV.n)], inc=sG)
            for nm, dstb in (("b_bkrow", bkbc), ("b_bvrow", bvbcB)):
                P.dma("gpsimd", rowtmp[0:1, 0:D], dr[nm][:], waits=[(sV, sV.n)], inc=sG)
                gg = sG.n
                for half in range(2):
                    P.op("tensor", lambda e, half=half: e.matmul(
                        pbank[half][:, 0:512], ones_rowb,
                        rowtmp[0:1, half * 512:(half + 1) * 512], start=True, stop=True),
                        waits=[(sG, gg)] + bwait(half), inc=sT)
                    P.op("vector", lambda e, half=half, dstb=dstb: e.tensor_copy(
                        dstb[:, half * 512:(half + 1) * 512], pbank[half][:, 0:512]),
                        waits=[(sT, sT.n)], inc=sV)
                    bfree(half, sV)
            P.barrier(ALLSEMS)

            iters = [(b, it) for b in range(B) for it in range(NTT)]
            GI = len(iters)
            proj_done = {-1: 0, -2: 0}
            stats_done = {-1: 0}
            vln = {-1: 0, -2: 0}
            slot_dma = [0, 0, 0, 0]
            slot_i = [0]

            def emit_ln1(g):
                b, it = iters[g]
                src_ap = src_of(b)
                cx = cxs[g % 2]
                w = [(sT, proj_done[g - 2]), (sV, vln[g - 2])]
                stats_done[g] = ln_tile(
                    src_ap.tensor, src_ap.offset, it, cx["xt8"], xsq8,
                    cx["xn8"], rows, bcr, bcmr, xld_waits=w,
                    xsem=sNp[g % 2], xsq_waits=[(sT, stats_done[g - 1])])
                vln[g] = sV.n

            def store_evac(psum_ap, bias_fn, dst_ap_flat):
                si = slot_i[0] % 4
                slot_i[0] += 1
                prst = prsts[si]
                P.op("vector", lambda e, prst=prst, psum_ap=psum_ap: bias_fn(e, prst),
                     waits=[(sT, sT.n), (sSs[si], slot_dma[si])] + self_edge(sV), inc=sV)
                P.dma("sync", dst_ap_flat, prst[:, 0:dst_ap_flat.ap[-1][1]],
                      waits=[(sV, sV.n)], inc=sSs[si])
                slot_dma[si] = sSs[si].n

            emit_ln1(0)
            for g in range(GI):
                b, it = iters[g]
                t0 = it * TT
                cx = cxs[g % 2]
                xn8 = cx["xn8"]
                if True:
                    for ec in range(ND):
                        bi = 4 + (ec % 2)
                        for dc in range(ND):
                            P.op("tensor", lambda e, ec=ec, dc=dc, bi=bi, xn8=xn8: e.matmul(
                                pbank[bi][:, 0:TT],
                                wqkv[:, dc * 3 * D + ec * 128: dc * 3 * D + (ec + 1) * 128],
                                xn8[:, dc * TT:(dc + 1) * TT],
                                start=(dc == 0), stop=(dc == ND - 1)),
                                waits=(([(sV, vln[g])] + bwait(bi)) if dc == 0 else ()),
                                inc=(sT if dc == ND - 1 else None))
                        bi_ = bi

                        def qbias(e, prst, bi=bi_, ec=ec):
                            return e.tensor_scalar(prst[:, 0:TT], pbank[bi][:, 0:TT],
                                                   bqB[:, ec:ec + 1], None, ALU.add)
                        store_evac(None, qbias,
                                   bass.AP(q_h.tensor, b * D * S + ec * 128 * S + t0,
                                           [[S, 128], [1, TT]]))
                        bfree(bi, sV)
                    for which, dsth, bbc in ((1, k_h, bkbc), (2, v_h, bvbcB)):
                        for tc in range(TT // 128):
                            for half in range(2):
                                bi = 4 + half
                                for dc in range(ND):
                                    P.op("tensor", lambda e, tc=tc, half=half, dc=dc,
                                         bi=bi, which=which, xn8=xn8: e.matmul(
                                             pbank[bi][:, 0:512],
                                             xn8[:, dc * TT + tc * 128: dc * TT + (tc + 1) * 128],
                                             wqkv[:, dc * 3 * D + which * D + half * 512:
                                                  dc * 3 * D + which * D + (half + 1) * 512],
                                             start=(dc == 0), stop=(dc == ND - 1)),
                                         waits=(bwait(bi) if dc == 0 else ()),
                                         inc=(sT if dc == ND - 1 else None))

                                def kvbias(e, prst, bi=bi, half=half, bbc=bbc):
                                    return e.scalar_tensor_tensor(
                                        prst[:, 0:512], pbank[bi][:, 0:512], 1.0,
                                        bbc[:, half * 512:(half + 1) * 512],
                                        ALU.mult, ALU.add)
                                store_evac(None, kvbias,
                                           bass.AP(dsth.tensor,
                                                   b * S * D + (t0 + tc * 128) * D + half * 512,
                                                   [[D, 128], [1, 512]]))
                                bfree(bi, sV)
                    proj_done[g] = sT.n
                    if g + 1 < GI:
                        emit_ln1(g + 1)
            P.barrier(ALLSEMS)
            AR.pop()
            if ATTNB_PHASE_LIMIT < 2:
                return

            AR.push()
            kbuf = AR.alloc(NT * D)
            fbufs = [AR.alloc(2 * NT * 128) for _ in range(2)]
            gchs = [AR.alloc(2 * D) for _ in range(2)]
            pch = AR.alloc(2 * D)
            t1b = AR.alloc(4 * D, F32)
            cbufs = [AR.alloc(2 * S) for _ in range(2)]
            pqbs = [AR.alloc(2 * 512) for _ in range(2)]
            kst = AR.alloc(512)
            NFC = NF2 // 128

            for b in range(B):
                for tc in range(NT):
                    P.dma("gpsimd", kbuf[:, tc * D:(tc + 1) * D],
                          bass.AP(k_h.tensor, b * S * D + tc * 128 * D, [[D, 128], [1, D]]),
                          waits=([(sT, sT.n), (sV, sV.n)] if tc == 0 else ()), inc=sG)
                P.barrier(ALLSEMS)

                fld = {}
                mmd = {}
                prd = {}
                pst = {-1: 0}
                def issue_loads(fc):
                    pb = fc % 2
                    w = []
                    if fc >= 2:
                        w = [(sT, mmd[fc - 2])]
                    P.dma("sync", fbufs[pb][:, 0:NT * 128],
                          bass.AP(Fc_h.tensor, fc * 128,
                                  [[NF2, 128], [128 * NF2, NT], [1, 128]]),
                          waits=w, inc=sFp[pb])
                    P.dma("sync", fbufs[pb][:, NT * 128:2 * NT * 128],
                          bass.AP(Fs_h.tensor, fc * 128,
                                  [[NF2, 128], [128 * NF2, NT], [1, 128]]),
                          inc=sFp[pb])
                    P.dma("gpsimd", gchs[pb][:, 0:D],
                          bass.AP(Gr_h.tensor, fc * 128 * D, [[D, 128], [1, D]]),
                          waits=(w if fc >= 2 else []) + ([(sV, prd[fc - 2])] if fc >= 2 else []),
                          inc=sGp[pb])
                    P.dma("gpsimd", gchs[pb][:, D:2 * D],
                          bass.AP(Gi_h.tensor, fc * 128 * D, [[D, 128], [1, D]]),
                          inc=sGp[pb])
                    fld[fc] = (sFp[pb].n, sGp[pb].n)

                issue_loads(0)
                issue_loads(1)
                for fc in range(NFC):
                    pb = fc % 2
                    bk = 4 * (fc % 2)
                    for ri in range(2):
                        for half in range(2):
                            bi = bk + 2 * ri + half
                            for tc in range(NT):
                                P.op("tensor", lambda e, ri=ri, half=half, tc=tc, bi=bi, pb=pb:
                                     e.matmul(
                                         pbank[bi][:, 0:512],
                                         fbufs[pb][:, ri * NT * 128 + tc * 128:
                                                   ri * NT * 128 + (tc + 1) * 128],
                                         kbuf[:, tc * D + half * 512:
                                              tc * D + (half + 1) * 512],
                                         start=(tc == 0), stop=(tc == NT - 1)),
                                     waits=(([(sFp[pb], fld[fc][0])] + bwait(bi))
                                            if tc == 0 else ()),
                                     inc=(sT if tc == NT - 1 else None))
                    mmd[fc] = sT.n
                    for half in range(2):
                        P.op("vector", lambda e, half=half, bk=bk, pb=pb: e.tensor_mul(
                            t1b[:, half * 512:(half + 1) * 512],
                            pbank[bk + half][:, 0:512],
                            gchs[pb][:, half * 512:(half + 1) * 512]),
                            waits=[(sT, mmd[fc]), (sGp[pb], fld[fc][1]), (sO2, pst[fc - 1])]
                            + self_edge(sV), inc=sV)
                        P.op("vector", lambda e, half=half, bk=bk, pb=pb: e.tensor_mul(
                            t1b[:, 1024 + half * 512:1024 + (half + 1) * 512],
                            pbank[bk + 2 + half][:, 0:512],
                            gchs[pb][:, D + half * 512: D + (half + 1) * 512]),
                            waits=self_edge(sV), inc=sV)
                        P.op("vector", lambda e, half=half: e.tensor_sub(
                            pch[:, half * 512:(half + 1) * 512],
                            t1b[:, half * 512:(half + 1) * 512],
                            t1b[:, 1024 + half * 512:1024 + (half + 1) * 512]),
                            waits=self_edge(sV), inc=sV)
                        P.op("vector", lambda e, half=half, bk=bk, pb=pb: e.tensor_mul(
                            t1b[:, half * 512:(half + 1) * 512],
                            pbank[bk + half][:, 0:512],
                            gchs[pb][:, D + half * 512: D + (half + 1) * 512]),
                            waits=self_edge(sV), inc=sV)
                        P.op("vector", lambda e, half=half, bk=bk, pb=pb: e.tensor_mul(
                            t1b[:, 1024 + half * 512:1024 + (half + 1) * 512],
                            pbank[bk + 2 + half][:, 0:512],
                            gchs[pb][:, half * 512:(half + 1) * 512]),
                            waits=self_edge(sV), inc=sV)
                        P.op("vector", lambda e, half=half: e.tensor_add(
                            pch[:, D + half * 512: D + (half + 1) * 512],
                            t1b[:, half * 512:(half + 1) * 512],
                            t1b[:, 1024 + half * 512:1024 + (half + 1) * 512]),
                            waits=self_edge(sV), inc=sV)
                    prd[fc] = sV.n
                    for bi in range(bk, bk + 4):
                        bfree(bi, sV)
                    P.dma("scalar",
                          bass.AP(Pr_h.tensor, fc * 128 * D, [[D, 128], [1, D]]),
                          pch[:, 0:D], waits=[(sV, prd[fc])], inc=sO2)
                    P.dma("scalar",
                          bass.AP(Pi_h.tensor, fc * 128 * D, [[D, 128], [1, D]]),
                          pch[:, D:2 * D], inc=sO2)
                    pst[fc] = sO2.n
                    if fc + 2 < NFC:
                        issue_loads(fc + 2)
                P.barrier(ALLSEMS)

                NTH = max(1, NT // 8)
                for eh in range(2):
                    for th in range(NTH):
                        tcs = list(range(th * 8, min(NT, th * 8 + 8)))
                        ild = {}
                        imm = {}

                        def issue_iloads(fc, eh=eh):
                            pb = fc % 2
                            w = [(sT, imm[fc - 2])] if fc >= 2 else [(sT, sT.n)]
                            P.dma("sync", cbufs[pb][:, 0:S],
                                  bass.AP(Cr_h.tensor, fc * 128 * S, [[S, 128], [1, S]]),
                                  waits=w, inc=sCp[pb])
                            P.dma("sync", cbufs[pb][:, S:2 * S],
                                  bass.AP(Ci_h.tensor, fc * 128 * S, [[S, 128], [1, S]]),
                                  inc=sCp[pb])
                            P.dma("gpsimd", pqbs[pb][:, 0:512],
                                  bass.AP(Pr_h.tensor, fc * 128 * D + eh * 512,
                                          [[D, 128], [1, 512]]),
                                  waits=w, inc=sPp[pb])
                            P.dma("gpsimd", pqbs[pb][:, 512:1024],
                                  bass.AP(Pi_h.tensor, fc * 128 * D + eh * 512,
                                          [[D, 128], [1, 512]]),
                                  inc=sPp[pb])
                            ild[fc] = (sCp[pb].n, sPp[pb].n)

                        issue_iloads(0)
                        issue_iloads(1)
                        for fc in range(NFC):
                            pb = fc % 2
                            for ti, tc in enumerate(tcs):
                                for ri in range(2):
                                    wts = []
                                    if ti == 0 and ri == 0:
                                        wts += [(sCp[pb], ild[fc][0]), (sPp[pb], ild[fc][1])]
                                    if fc == 0 and ri == 0:
                                        wts += bwait(ti)
                                    P.op("tensor", lambda e, tc=tc, ri=ri, ti=ti, fc=fc, pb=pb:
                                         e.matmul(
                                             pbank[ti][:, 0:512],
                                             cbufs[pb][:, ri * S + tc * 128:
                                                       ri * S + (tc + 1) * 128],
                                             pqbs[pb][:, ri * 512:(ri + 1) * 512],
                                             start=(fc == 0 and ri == 0),
                                             stop=(fc == NFC - 1 and ri == 1)),
                                         waits=wts,
                                         inc=(sT if (ti == len(tcs) - 1 and ri == 1) else None))
                            imm[fc] = sT.n
                            if fc + 2 < NFC:
                                issue_iloads(fc + 2)
                        tinv = sT.n
                        for ti, tc in enumerate(tcs):
                            P.op("vector", lambda e, tc=tc, ti=ti: e.tensor_copy(
                                kst[:, 0:512], pbank[ti][:, 0:512]),
                                waits=[(sT, tinv), (sO, sO.n)] + self_edge(sV), inc=sV)
                            P.dma("sync",
                                  bass.AP(ktot_h.tensor, b * S * D + tc * 128 * D + eh * 512,
                                          [[D, 128], [1, 512]]),
                                  kst[:, 0:512], waits=[(sV, sV.n)], inc=sO)
                            bfree(ti, sV)
                P.barrier(ALLSEMS)
            AR.pop()
            if ATTNB_PHASE_LIMIT < 3:
                return

            AR.push()
            kslcs = [AR.alloc(NT * 32) for _ in range(2)]
            vslcs = [AR.alloc(NT * 32) for _ in range(2)]
            kvbs = [AR.alloc(NT * 256) for _ in range(2)]
            strips = [AR.alloc(S) for _ in range(2)]
            qreps = [AR.alloc(S) for _ in range(2)]
            qyb = AR.alloc(2 * S)
            ysb = AR.alloc(512)
            drepc = AR.alloc(2 * 128, F32)
            diagm = AR.alloc(128)
            cJb = AR.alloc(16)
            stg2 = AR.alloc(2 * 2 * S)
            P.dma("gpsimd", drepc[:], dr["b_Drep"][:], inc=sG)
            P.dma("gpsimd", diagm[:], dr["c_diag"][:], inc=sG)
            P.dma("gpsimd", cJb[:], dr["c_J"][:], inc=sG)
            P.barrier(ALLSEMS)

            NQQ = S // 512 if S >= 512 else 1
            NW = min(512, S)
            gld = {}
            kv_done = {-1: 0, -2: 0}
            conv_done = {-1: 0, -2: 0}
            qy_done = {-1: 0, -2: 0}

            def p3_loads(c):
                pb = c % 2
                sl = sLp[pb]
                wk = [(sV, kv_done[c - 2])]
                for b in range(B):
                    P.dma("gpsimd",
                          bass.AP(kslcs[pb].tensor, kslcs[pb].offset + b * 8,
                                  [[kslcs[pb].ap[0][0], 128], [32, NT], [1, 8]]),
                          bass.AP(ktot_h.tensor, b * S * D + c * 8,
                                  [[D, 128], [128 * D, NT], [1, 8]]),
                          waits=(wk if b == 0 else ()), inc=sl)
                    P.dma("gpsimd",
                          bass.AP(vslcs[pb].tensor, vslcs[pb].offset + b * 8,
                                  [[vslcs[pb].ap[0][0], 128], [32, NT], [1, 8]]),
                          bass.AP(v_h.tensor, b * S * D + c * 8,
                                  [[D, 128], [128 * D, NT], [1, 8]]),
                          inc=sl)
                P.dma("gpsimd", strips[pb][:],
                      bass.AP(dr["b_strips"].tensor, c * 128 * S, [[S, 128], [1, S]]),
                      waits=[(sT, conv_done[c - 2])], inc=sl)
                for jj in range(4):
                    P.dma("gpsimd", qreps[pb][32 * jj:32 * jj + 32, :],
                          bass.AP(q_h.tensor, c * 8 * S,
                                  [[S, 8], [D * S, 4], [1, S]]),
                          waits=([(sV, qy_done[c - 2])] if jj == 0 else ()), inc=sl)
                gld[c] = sl.n

            p3_loads(0)
            for c in range(128):
                pb = c % 2
                kslc, vslc, kvb = kslcs[pb], vslcs[pb], kvbs[pb]
                stripb, qrep = strips[pb], qreps[pb]
                P.op("vector", lambda e, c=c, stripb=stripb: e.scalar_tensor_tensor(
                    stripb[:, 0:128], diagm[:], drepc[:, c:c + 1], stripb[:, 0:128],
                    ALU.mult, ALU.add),
                    waits=[(sLp[c % 2], gld[c])] + self_edge(sV), inc=sV)
                P.op("vector", lambda e: e.nop(), waits=[(sT, conv_done[c - 2])], inc=sV)
                for tc in range(NT):
                    P.op("vector", lambda e, tc=tc, kvb=kvb, kslc=kslc, vslc=vslc: e.tensor_mul(
                        kvb[:, tc * 256:(tc + 1) * 256].rearrange(
                            "p (j i b) -> p j i b", j=8, i=8, b=4),
                        bass.AP(kslc.tensor, kslc.offset + tc * 32,
                                [[kslc.ap[0][0], 128], [0, 8], [1, 8], [8, 4]]),
                        bass.AP(vslc.tensor, vslc.offset + tc * 32,
                                [[vslc.ap[0][0], 128], [1, 8], [0, 8], [8, 4]])),
                        waits=self_edge(sV), inc=sV)
                vkv = sV.n
                kv_done[c] = vkv
                if c + 1 < 128:
                    p3_loads(c + 1)
                for cc in range(2):
                    for s_ in range(NT):
                        for qq in range(max(1, (NT * 128 - 128 * s_ + 511) // 512)):
                            qqa = s_ // 4 + qq if S >= 512 else 0
                            if S >= 512:
                                t0q = max(128 * s_, 512 * qqa)
                                tend = 512 * (qqa + 1)
                            else:
                                t0q = 128 * s_
                                tend = S
                            if t0q >= tend:
                                continue
                            NWW = tend - t0q
                            bi = (qqa % 4) if S >= 512 else 0
                            P.op("tensor", lambda e, cc=cc, s_=s_, qqa=qqa, t0q=t0q,
                                 NWW=NWW, bi=bi, kvb=kvb, stripb=stripb: e.matmul(
                                     pbank[bi][:, (t0q - 512 * qqa) if S >= 512 else t0q:
                                                ((t0q - 512 * qqa) if S >= 512 else t0q) + NWW],
                                     kvb[:, s_ * 256 + cc * 128: s_ * 256 + cc * 128 + 128],
                                     stripb[:, t0q - 128 * s_: t0q - 128 * s_ + NWW],
                                     start=(s_ == 0),
                                     stop=(s_ == min(NT - 1, 4 * qqa + 3) if S >= 512
                                           else s_ == NT - 1)),
                                 waits=(([(sV, vkv)] + bwait(bi)) if s_ == 0 else ()),
                                 inc=sT)
                    tcv = sT.n
                    for qqa in range(NQQ):
                        bi = qqa % 4
                        P.op("scalar", lambda e, qqa=qqa, bi=bi: e.copy(
                            ysb[:, 0:NW], pbank[bi][:, 0:NW]),
                            waits=[(sT, tcv), (sV, sV.n)] + self_edge(sA), inc=sA)
                        P.op("vector", lambda e, cc=cc, qqa=qqa, qrep=qrep: e.tensor_mul(
                            qyb[:, cc * S + qqa * NW: cc * S + qqa * NW + NW],
                            ysb[:, 0:NW], qrep[:, qqa * NW: qqa * NW + NW]),
                            waits=[(sA, sA.n)] + self_edge(sV), inc=sV)
                        bfree(bi, sV)
                conv_done[c] = sT.n
                qy_done[c] = sV.n
                vqy = sV.n
                for cc in range(2):
                    for qqa in range(NQQ):
                        bi = 4 + (qqa % 2)
                        P.op("tensor", lambda e, cc=cc, qqa=qqa, bi=bi: e.matmul(
                            pbank[bi][0:16, 0:NW], cJb[:],
                            qyb[:, cc * S + qqa * NW: cc * S + qqa * NW + NW],
                            start=True, stop=True),
                            waits=[(sV, vqy)] + bwait(bi), inc=sT)
                        P.op("scalar", lambda e, c=c, cc=cc, qqa=qqa, bi=bi: e.copy(
                            stg2[0:16, (c % 2) * 2 * S + cc * S + qqa * NW:
                                 (c % 2) * 2 * S + cc * S + qqa * NW + NW],
                            pbank[bi][0:16, 0:NW]),
                            waits=[(sT, sT.n), (sO, sO.n)]
                            + self_edge(sA), inc=sA)
                        bfree(bi, sA)
                if c % 2 == 1:
                    vst = sA.n
                    for b in range(B):
                        for j in range(8):
                            cc, jl = j // 4, j % 4
                            m = jl * 4 + b
                            P.dma("sync",
                                  bass.AP(yb_h.tensor,
                                          b * D * S + (j * 128 + (c - 1)) * S,
                                          [[S, 2], [1, S]]),
                                  bass.AP(stg2.tensor, stg2.offset + m * stg2.ap[0][0] + cc * S,
                                          [[stg2.ap[0][0], 1], [2 * S, 2], [1, S]]),
                                  waits=[(sA, vst)], inc=sO)
            P.barrier(ALLSEMS)
            AR.pop()
            if ATTNB_PHASE_LIMIT < 4:
                return

            AR.push()
            wo = AR.alloc(ND * D)
            ybuf = AR.alloc(ND * S)
            xt8 = AR.alloc(ND * TT)
            boB = AR.alloc(2 * ND, F32)
            xo32 = AR.alloc(2 * TT, F32)
            stg3 = AR.alloc(TT)
            P.dma("sync", wo[:], woB_h[:], inc=sQ)
            P.dma("gpsimd", boB[:], dr["b_bo"][:], inc=sG)
            P.barrier(ALLSEMS)
            for b in range(B):
                src_ap = src_of(b)
                dst_ap = dst_of(b)
                P.dma("sync", ybuf[:],
                      bass.AP(yb_h.tensor, b * D * S, [[S, 128], [128 * S, ND], [1, S]]),
                      waits=[(sT, sT.n), (sO, sO.n)], inc=sQ)
                P.barrier(ALLSEMS)
                for iq in range(NTT):
                    t0 = iq * TT
                    P.dma("gpsimd", xt8[:],
                          bass.AP(src_ap.tensor, src_ap.offset + t0,
                                  [[S, 128], [128 * S, ND], [1, TT]]),
                          waits=[(sV, sV.n), (sO, sO.n)], inc=sG)
                    gx = sG.n
                    for oc in range(ND):
                        bi = 4 + (oc % 2)
                        for ecc in range(ND):
                            P.op("tensor", lambda e, oc=oc, ecc=ecc, bi=bi, t0=t0: e.matmul(
                                pbank[bi][:, 0:TT],
                                wo[:, ecc * D + oc * 128: ecc * D + (oc + 1) * 128],
                                ybuf[:, ecc * S + t0: ecc * S + t0 + TT],
                                start=(ecc == 0), stop=(ecc == ND - 1)),
                                waits=(bwait(bi) if ecc == 0 else ()),
                                inc=(sT if ecc == ND - 1 else None))
                        tdone = sT.n
                        stg = xo32 if final else stg3
                        P.op("vector", lambda e, oc=oc, bi=bi, stg=stg:
                             e.scalar_tensor_tensor(
                                 stg[:, 0:TT], pbank[bi][:, 0:TT], boB[:, oc:oc + 1],
                                 xt8[:, oc * TT:(oc + 1) * TT], ALU.add, ALU.add),
                             waits=[(sT, tdone), (sG, gx), (sO, sO.n)] + self_edge(sV), inc=sV)
                        bfree(bi, sV)
                        P.dma("sync",
                              bass.AP(dst_ap.tensor, dst_ap.offset + oc * 128 * S + t0,
                                      [[S, 128], [1, TT]]),
                              stg[:, 0:TT], waits=[(sV, sV.n)], inc=sO)
            P.barrier(ALLSEMS)
            AR.pop()

        def fwd_dft_local(rhsbuf, fbuf, cb):
            for fc in range(NF2 // 128):
                P.dma("sync", fbuf[:, 0:NT * 128],
                      bass.AP(Fc_h.tensor, fc * 128,
                              [[NF2, 128], [128 * NF2, NT], [1, 128]]),
                      waits=[(sV, sV.n), (sA, sA.n), (sT, sT.n)], inc=sQ)
                P.dma("sync", fbuf[:, NT * 128:2 * NT * 128],
                      bass.AP(Fs_h.tensor, fc * 128,
                              [[NF2, 128], [128 * NF2, NT], [1, 128]]),
                      inc=sQ)
                gq = sQ.n
                for ri in range(2):
                    for half in range(2):
                        bi = 2 * ri + half
                        for tc in range(NT):
                            P.op("tensor", lambda e, ri=ri, half=half, tc=tc, bi=bi:
                                 e.matmul(
                                     pbank[bi][:, 0:512],
                                     fbuf[:, ri * NT * 128 + tc * 128:
                                          ri * NT * 128 + (tc + 1) * 128],
                                     rhsbuf[:, tc * D + half * 512:
                                            tc * D + (half + 1) * 512],
                                     start=(tc == 0), stop=(tc == NT - 1)),
                                 waits=(([(sQ, gq)] + bwait(bi)) if tc == 0 else ()),
                                 inc=(sT if tc == NT - 1 else None))
                tfd = sT.n
                cb(fc, tfd)

        if "ffn1" in SUBS or "ffn2" in SUBS:
            ffn_prep()
        if "attn_a" in SUBS:
            attn_prep()
        if "attn_b" in SUBS:
            attn_b_prep()

        chain = [n for n in ("attn_a", "ffn1", "attn_b", "ffn2") if n in SUBS]
        xbufs = [x_res, x_res2]
        for li, name in enumerate(chain):
            final = (li == len(chain) - 1)

            def src_of(b, li=li):
                return dr["x_in"][b] if li == 0 else xbufs[(li - 1) % 2][b]

            def dst_of(b, final=final, li=li):
                return dr["y"][b] if final else xbufs[li % 2][b]

            if name.startswith("ffn"):
                for b in range(B):
                    ffn_call(b, src_of(b), dst_of(b), final)
            elif name == "attn_a":
                for b in range(B):
                    attn_a_call(b, src_of(b), dst_of(b), final)
            elif name == "attn_b":
                attn_b_call(src_of, dst_of, final)

        P.op("sync", None, waits=[(sO, sO.n), (sQ, sQ.n)])
        P.emit_all(block)

    return nc


def butterfly_coef_arrays(weights, transposed):
    NSTG, NPAIR = weights.shape[0], weights.shape[1]
    DH = 2 * NPAIR
    SELF = np.zeros((NSTG, DH), np.float32)
    OTHER = np.zeros((NSTG, DH), np.float32)
    for st in range(NSTG):
        s = 1 << st
        k = np.arange(NPAIR)
        fa = (k // s) * 2 * s + (k % s)
        fb = fa + s
        w = np.asarray(weights[st], np.float32)
        if not transposed:
            SELF[st, fa] = w[:, 0, 0]
            OTHER[st, fa] = w[:, 0, 1]
            SELF[st, fb] = w[:, 1, 1]
            OTHER[st, fb] = w[:, 1, 0]
        else:
            SELF[st, fa] = w[:, 0, 0]
            OTHER[st, fa] = w[:, 1, 0]
            SELF[st, fb] = w[:, 1, 1]
            OTHER[st, fb] = w[:, 0, 1]
    return SELF, OTHER


def rep_tiles(SELF, OTHER, NJ):
    repS = np.broadcast_to(SELF[:7].reshape(7, NJ, 1, 128), (7, NJ, 128, 128))
    repO = np.broadcast_to(OTHER[:7].reshape(7, NJ, 1, 128), (7, NJ, 128, 128))
    return (np.ascontiguousarray(repS, dtype=np.float32),
            np.ascontiguousarray(repO, dtype=np.float32))


def host_prep_core(inputs, cfg, xcore):
    d = {}
    d["x_in"] = np.ascontiguousarray(np.asarray(xcore, np.float32).transpose(0, 2, 1))
    d["c_ident"] = np.eye(128, dtype=np.float32)
    SUBS = cfg["sublayers"]
    if "attn_a" in SUBS:
        D = cfg["D"]
        d["a_WqkvT"] = np.ascontiguousarray(np.asarray(inputs["L0_attn_a_Wqkv"], np.float32).T)
        d["a_WoT"] = np.ascontiguousarray(np.asarray(inputs["L0_attn_a_Wo"], np.float32).T)
        bqkv = np.asarray(inputs["L0_attn_a_bqkv"], np.float32)
        d["a_bq"] = np.ascontiguousarray(bqkv[0:D].reshape(8, 128).T)
        d["a_bk"] = np.ascontiguousarray(bqkv[D:2 * D].reshape(8, 128).T)
        d["a_bvrow"] = np.ascontiguousarray(bqkv[2 * D:3 * D].reshape(1, D))
        d["a_bo"] = np.ascontiguousarray(
            np.asarray(inputs["L0_attn_a_bo"], np.float32).reshape(8, 128).T)
        d["a_wq"] = np.ascontiguousarray(
            np.tile(np.asarray(inputs["L0_attn_a_wq"], np.float32), 2).reshape(128, 1))
        d["a_wk"] = np.ascontiguousarray(
            np.tile(np.asarray(inputs["L0_attn_a_wk"], np.float32), 2).reshape(128, 1))
        rs = np.zeros((128, 2), np.float32)
        rs[0:64, 0] = 1.0
        rs[64:128, 1] = 1.0
        d["c_rsum"] = rs
        d["c_sel2"] = np.ascontiguousarray(rs.T)
    if "attn_b" in SUBS:
        D = cfg["D"]
        S = cfg["S"]
        NF = 2 * S
        NF2 = S + 128
        Wq = np.asarray(inputs["L0_attn_b_Wq"], np.float32)
        Wk = np.asarray(inputs["L0_attn_b_Wk"], np.float32)
        Wv = np.asarray(inputs["L0_attn_b_Wv"], np.float32)
        d["b_WqkvT"] = np.ascontiguousarray(
            np.concatenate([Wq.T, Wk.T, Wv.T], axis=1))
        d["b_WoT"] = np.ascontiguousarray(np.asarray(inputs["L0_attn_b_Wo"], np.float32).T)
        d["b_bq"] = np.ascontiguousarray(
            np.asarray(inputs["L0_attn_b_bq"], np.float32).reshape(8, 128).T)
        d["b_bkrow"] = np.asarray(inputs["L0_attn_b_bk"], np.float32).reshape(1, D)
        d["b_bvrow"] = np.asarray(inputs["L0_attn_b_bv"], np.float32).reshape(1, D)
        d["b_bo"] = np.ascontiguousarray(
            np.asarray(inputs["L0_attn_b_bo"], np.float32).reshape(8, 128).T)
        t = np.arange(S, dtype=np.float64)
        f = np.arange(NF2, dtype=np.float64)
        valid = (f <= S).astype(np.float64)
        ang = np.outer(t, f) * (2.0 * np.pi / NF)
        d["b_Fc"] = (np.cos(ang) * valid).astype(np.float32)
        d["b_Fs"] = (-np.sin(ang) * valid).astype(np.float32)
        w = np.where((f == 0) | (f == S), 1.0, 2.0) * valid
        angT = np.outer(f, t) * (2.0 * np.pi / NF)
        d["b_Cr"] = ((w[:, None] / NF) * np.cos(angT)).astype(np.float32)
        d["b_Ci"] = (-(w[:, None] / NF) * np.sin(angT)).astype(np.float32)
        d["b_kernT"] = np.ascontiguousarray(
            np.asarray(inputs["L0_attn_b_ssm_k_kernel"], np.float32).T)
        d["b_kDrow"] = np.asarray(inputs["L0_attn_b_ssm_k_D"], np.float32).reshape(1, D)
        kern = np.asarray(inputs["L0_attn_b_ssm_kernel"], np.float32)
        strips = np.zeros((128, 128, S), np.float32)
        kp = np.zeros((128, 127 + S), np.float32)
        kp[:, 127:] = kern
        for bp in range(128):
            strips[:, bp, :] = kp[:, 127 - bp:127 - bp + S]
        d["b_strips"] = strips
        d["b_Drep"] = np.ascontiguousarray(
            np.broadcast_to(np.asarray(inputs["L0_attn_b_D"], np.float32)[None, :],
                            (128, 128)))
        d["c_diag"] = np.eye(128, dtype=np.float32)
        J = np.zeros((128, 16), np.float32)
        for jl in range(4):
            for i in range(8):
                for b in range(4):
                    J[jl * 32 + i * 4 + b, jl * 4 + b] = 1.0
        d["c_J"] = J
    if "ffn1" in SUBS or "ffn2" in SUBS:
        d["f_WexpT"] = np.ascontiguousarray(np.asarray(inputs["L0_ffn_W_exp"], np.float32).T)
        d["f_Wcon"] = np.ascontiguousarray(np.asarray(inputs["L0_ffn_W_con"], np.float32))
        SU, OU = butterfly_coef_arrays(np.asarray(inputs["L0_ffn_up_weights"]), False)
        SD, OD = butterfly_coef_arrays(np.asarray(inputs["L0_ffn_down_weights"]), True)
        def coef_t(A):
            return np.ascontiguousarray(A.reshape(12, -1, 128).transpose(0, 2, 1))
        d["f_selfU"], d["f_otherU"] = coef_t(SU), coef_t(OU)
        d["f_selfD"], d["f_otherD"] = coef_t(SD), coef_t(OD)
        d["f_repSU"], d["f_repOU"] = rep_tiles(SU, OU, cfg["NJ"])
        d["f_repSD"], d["f_repOD"] = rep_tiles(SD, OD, cfg["NJ"])
        def vec_t(v, nj):
            return np.ascontiguousarray(np.asarray(v, np.float32).reshape(nj, 128).T)
        d["f_bexp"] = vec_t(inputs["L0_ffn_b_exp"], cfg["NJ"])
        d["f_upbias"] = vec_t(inputs["L0_ffn_up_bias"], cfg["NJ"])
        d["f_bcon"] = vec_t(inputs["L0_ffn_b_con"], 8)
        d["f_dbias"] = vec_t(inputs["L0_ffn_down_bias"], 8)
    return d


def kernel(**inputs):
    cfg = make_cfg()
    x = np.asarray(inputs["L0_attn_a_x"], np.float32)
    nc = build_program(cfg)
    in_maps = [host_prep_core(inputs, cfg, x[c * 4:(c + 1) * 4]) for c in range(8)]
    res = run_bass_kernel_spmd(nc, in_maps, core_ids=list(range(8)))
    outs = [np.ascontiguousarray(res.results[c]["y"].transpose(0, 2, 1)) for c in range(8)]
    return np.concatenate(outs, axis=0).astype(np.float32)


# version 11
# speedup vs baseline: 1.5726x; 1.0451x over previous
import contextlib
import numpy as np
import sys
sys.path.insert(0, '/opt/trn_rl_repo')
import concourse.bass as bass
import concourse.mybir as mybir
from concourse.bass_utils import run_bass_kernel_spmd

F32 = mybir.dt.float32
BF16 = mybir.dt.bfloat16
AF = mybir.ActivationFunctionType
ALU = mybir.AluOpType
AX = mybir.AxisListType


ATTNB_PHASE_LIMIT = 4
DEBUG_KTOT = False

def make_cfg(S=2048, B=4, sublayers=("attn_a", "ffn1", "attn_b", "ffn2")):
    return dict(S=S, B=B, D=1024, NT=S // 128, ND=8, DH=4096, NJ=32,
                TT=min(512, S), sublayers=tuple(sublayers))


class Sem:
    def __init__(self, handle, name):
        self.h, self.name, self.n = handle, name, 0


class Arena:
    def __init__(self, tensor, total_elems):
        self.t = tensor
        self.total = total_elems
        self.off = 0
        self.marks = []

    def push(self):
        self.marks.append(self.off)

    def pop(self):
        self.off = self.marks.pop()

    def alloc(self, elems_bf16, dtype=BF16):
        self.off = (self.off + 1) // 2 * 2
        start = self.off
        self.off += elems_bf16
        assert self.off <= self.total, f"SBUF arena overflow: {self.off} > {self.total}"
        view = self.t[:, start:start + elems_bf16]
        if dtype == F32:
            view = view.bitcast(F32)
        return view


class Prog:
    ENGINES = ("sync", "tensor", "vector", "scalar", "gpsimd")

    def __init__(self, nc):
        self.nc = nc
        self.streams = {e: [] for e in self.ENGINES}

    def op(self, eng, fn=None, waits=(), inc=None, amt=1):
        if inc is not None:
            inc.n += amt
        want = [(s.h, v) for (s, v) in waits if v > 0]
        ih = inc.h if inc is not None else None

        def emit(engine, fn=fn, want=want, ih=ih, amt=amt):
            for h, v in want:
                engine.wait_ge(h, v)
            if fn is not None:
                instr = fn(engine)
                if ih is not None:
                    instr.then_inc(ih, amt)
        self.streams[eng].append(emit)
        return inc.n if inc is not None else None

    def dma(self, eng, out, in_, waits=(), inc=None):
        return self.op(eng, lambda e: e.dma_start(out, in_), waits=waits, inc=inc, amt=16)

    def barrier(self, sems):
        snap = [(s, s.n) for s in sems if s.n > 0]
        for eng in self.ENGINES:
            self.op(eng, None, waits=snap)

    def emit_all(self, block):
        for eng in self.ENGINES:
            stream = self.streams[eng]

            def make(s):
                def _(engine):
                    for f in s:
                        f(engine)
                return _
            getattr(block, eng)(make(stream))


def build_program(cfg):
    S, B, D = cfg["S"], cfg["B"], cfg["D"]
    NT, ND, DH, NJ, TT = cfg["NT"], cfg["ND"], cfg["DH"], cfg["NJ"], cfg["TT"]
    NTT = S // TT
    SUBS = cfg["sublayers"]

    nc = bass.Bass("TRN2", target_bir_lowering=False, detect_race_conditions=True)
    dr = {}

    def din(name, shape, dtype=F32):
        dr[name] = nc.dram_tensor(name, shape, dtype, kind="ExternalInput").ap()
        return dr[name]

    din("x_in", [B, D, S])
    din("c_ident", [128, 128])
    if "attn_a" in SUBS:
        din("a_WqkvT", [D, 3 * D])
        din("a_WoT", [D, D])
        din("a_bq", [128, 8])
        din("a_bk", [128, 8])
        din("a_bvrow", [1, D])
        din("a_wq", [128, 1])
        din("a_wk", [128, 1])
        din("a_bo", [128, 8])
        din("c_rsum", [128, 2])
        din("c_sel2", [2, 128])
    if "attn_b" in SUBS:
        NF2 = S + 128
        din("b_WqkvT", [D, 3 * D])
        din("b_WoT", [D, D])
        din("b_bq", [128, 8])
        din("b_bkrow", [1, D])
        din("b_bvrow", [1, D])
        din("b_bo", [128, 8])
        din("b_Fc", [S, NF2])
        din("b_Fs", [S, NF2])
        din("b_Cr", [NF2, S])
        din("b_Ci", [NF2, S])
        din("b_kernT", [S, D])
        din("b_kDrow", [1, D])
        din("b_strips", [128, 128, S])
        din("b_Drep", [128, 128])
        din("c_diag", [128, 128])
        din("c_J", [128, 16])
    if "ffn1" in SUBS or "ffn2" in SUBS:
        din("f_WexpT", [DH, D])
        din("f_Wcon", [DH, D])
        din("f_selfU", [12, 128, NJ])
        din("f_otherU", [12, 128, NJ])
        din("f_selfD", [12, 128, NJ])
        din("f_otherD", [12, 128, NJ])
        din("f_repSU", [7, NJ, 128, 128])
        din("f_repOU", [7, NJ, 128, 128])
        din("f_repSD", [7, NJ, 128, 128])
        din("f_repOD", [7, NJ, 128, 128])
        din("f_bexp", [128, NJ])
        din("f_upbias", [128, NJ])
        din("f_bcon", [128, ND])
        din("f_dbias", [128, ND])

    y_out = nc.dram_tensor("y", [B, D, S], F32, kind="ExternalOutput").ap()
    dr["y"] = y_out
    x_res = nc.dram_tensor("x_res", [B, D, S], BF16, kind="Internal").ap()
    x_res2 = nc.dram_tensor("x_res2", [B, D, S], BF16, kind="Internal").ap()
    wexp_h = nc.dram_tensor("wexp_h", [128, ND * DH], BF16, kind="Internal").ap()
    wcon_h = nc.dram_tensor("wcon_h", [128, NJ * D], BF16, kind="Internal").ap()
    wqkv_h = nc.dram_tensor("wqkv_h", [128, ND * 3 * D], BF16, kind="Internal").ap()
    woT_h = nc.dram_tensor("woT_h", [128, ND * D], BF16, kind="Internal").ap()
    NF2 = S + 128
    wqkvB_h = nc.dram_tensor("wqkvB_h", [128, ND * 3 * D], BF16, kind="Internal").ap()
    woB_h = nc.dram_tensor("woB_h", [128, ND * D], BF16, kind="Internal").ap()
    Fc_h = nc.dram_tensor("Fc_h", [S, NF2], BF16, kind="Internal").ap()
    Fs_h = nc.dram_tensor("Fs_h", [S, NF2], BF16, kind="Internal").ap()
    Cr_h = nc.dram_tensor("Cr_h", [NF2, S], BF16, kind="Internal").ap()
    Ci_h = nc.dram_tensor("Ci_h", [NF2, S], BF16, kind="Internal").ap()
    Gr_h = nc.dram_tensor("Gr_h", [NF2, D], BF16, kind="Internal").ap()
    Gi_h = nc.dram_tensor("Gi_h", [NF2, D], BF16, kind="Internal").ap()
    Pr_h = nc.dram_tensor("Pr_h", [NF2, D], BF16, kind="Internal").ap()
    Pi_h = nc.dram_tensor("Pi_h", [NF2, D], BF16, kind="Internal").ap()
    q_h = nc.dram_tensor("q_h", [B, D, S], BF16, kind="Internal").ap()
    ktot_h = nc.dram_tensor("ktot_h", [B, S, D], BF16,
                            kind="ExternalOutput" if DEBUG_KTOT else "Internal").ap()
    v_h = nc.dram_tensor("v_h", [B, S, D], BF16, kind="Internal").ap()
    k_h = nc.dram_tensor("k_h", [B, S, D], BF16, kind="Internal").ap()
    yb_h = nc.dram_tensor("yb_h", [B, D, S], BF16, kind="Internal").ap()

    P = Prog(nc)
    es = contextlib.ExitStack()
    with es:
        E = es.enter_context
        sems = {}

        def sem(name):
            sems[name] = Sem(E(nc.semaphore(name)), name)
            return sems[name]

        sQ = sem("sQ")
        sG = sem("sG")
        sT = sem("sT")
        sV = sem("sV")
        sA = sem("sA")
        sO = sem("sO")
        sG2 = sem("sG2")
        sO2 = sem("sO2")
        sSs = [sem("sSs%d" % i) for i in range(4)]
        sFp = [sem("sFp0"), sem("sFp1")]
        sGp = [sem("sGp0"), sem("sGp1")]
        sCp = [sem("sCp0"), sem("sCp1")]
        sPp = [sem("sPp0"), sem("sPp1")]
        sLp = [sem("sLp0"), sem("sLp1")]
        sNp = [sem("sNp0"), sem("sNp1")]
        ALLSEMS = ([sQ, sG, sT, sV, sA, sO, sG2, sO2] + sFp + sGp + sCp + sPp
                   + sLp + sNp + sSs)

        pbank = [E(nc.psum_tensor(f"pb{i}", [128, 512], F32)) for i in range(8)]
        bank_free = {}

        def bwait(i):
            return [bank_free[i]] if i in bank_free else []

        def bfree(i, s):
            bank_free[i] = (s, s.n)

        AR_TOTAL = 106400
        arena_t = E(nc.sbuf_tensor("arena", [128, AR_TOTAL], BF16))
        AR = Arena(arena_t, AR_TOTAL)

        ident = AR.alloc(128)
        ones_colb = AR.alloc(2)[:, 0:1]
        ones_rowb_t = AR.alloc(128)
        ones_rowf_t = AR.alloc(256, F32)
        fb1 = AR.alloc(2 * NJ, F32)
        fbo = AR.alloc(2 * ND, F32)

        block = E(nc.Block())

        P.op("vector", lambda e: e.memset(ones_colb[:], 1.0), inc=sV)
        P.op("vector", lambda e: e.memset(ones_rowb_t[:], 1.0), inc=sV)
        P.op("vector", lambda e: e.memset(ones_rowf_t[:], 1.0), inc=sV)
        ones_rowb = ones_rowb_t[0:1, :]
        ones_rowf = ones_rowf_t[0:1, :]
        P.dma("gpsimd", ident[:], dr["c_ident"][:], inc=sG)
        P.barrier(ALLSEMS)

        def self_edge(s):
            return [(s, s.n)]

        cstore = AR.alloc(2 * 16, F32)
        cslot = [0]

        def constf(value):
            sl = cstore[:, cslot[0]:cslot[0] + 1]
            cslot[0] += 1
            P.op("vector", lambda e, sl=sl, value=value: e.memset(sl, float(value)), inc=sV)
            nc.const_aps.register(F32, float(value), sl)
            return value

        constf(1e-5)
        constf(-0.5)
        constf(1.0 / D)
        constf(1e-6)
        constf(1.0 / 64.0)
        constf(0.5)
        constf(-0.4)
        P.barrier(ALLSEMS)

        def ffn_prep():
            AR.push()
            DT = 256
            wj_a = AR.alloc(NJ * DT)
            wj_b = AR.alloc(NJ * DT)
            wj_c = AR.alloc(NJ * DT)
            wjl = [wj_a, wj_c]
            wtmp = AR.alloc(DT)
            wtm2 = [wtmp, AR.alloc(DT)]
            coefS = AR.alloc(2 * 12 * NJ, F32)
            coefO = AR.alloc(2 * 12 * NJ, F32)
            creps = [(AR.alloc(128), AR.alloc(128)) for _ in range(2)]
            wts = [AR.alloc(128) for _ in range(4)]
            cmat = AR.alloc(NJ * 128)
            xm_a = AR.alloc(128)
            xm_b = AR.alloc(128)
            xm_t = AR.alloc(128)
            bvec = AR.alloc(NJ)
            bvec2 = AR.alloc(NJ)
            btmp = AR.alloc(2 * NJ, F32)

            def load_coefs(tag):
                for st in range(12):
                    P.dma("gpsimd", coefS[:, st * NJ:(st + 1) * NJ],
                          dr[f"f_self{tag}"][st], inc=sG)
                    P.dma("gpsimd", coefO[:, st * NJ:(st + 1) * NJ],
                          dr[f"f_other{tag}"][st], inc=sG)
                P.barrier(ALLSEMS)

            def build_cmats(rep_s, rep_o, stage_order):
                seq = [(jc, st) for jc in range(NJ) for st in stage_order]
                ld = {}
                dve_done = {-1: 0, -2: 0}

                def issue(q):
                    jc, st = seq[q]
                    pb = q % 2
                    off = (st * NJ + jc) * 128 * 128
                    P.dma("gpsimd", creps[pb][0][:],
                          bass.AP(dr[rep_s].tensor, off, [[128, 128], [1, 128]]),
                          waits=[(sV, dve_done[q - 2])], inc=sGp[pb])
                    P.dma("gpsimd", creps[pb][1][:],
                          bass.AP(dr[rep_o].tensor, off, [[128, 128], [1, 128]]),
                          inc=sGp[pb])
                    ld[q] = sGp[pb].n

                issue(0)
                issue(1)
                cur, nxt = xm_a, xm_b
                for q, (jc, st) in enumerate(seq):
                    pb = q % 2
                    crep_a, crep_b = creps[pb]
                    if st == stage_order[0]:
                        P.op("vector", lambda e: e.tensor_copy(xm_a[:], ident[:]),
                             waits=self_edge(sV) + [(sT, sT.n)], inc=sV)
                        cur, nxt = xm_a, xm_b
                    s_ = 1 << st
                    nb = 128 // (2 * s_)
                    shp = dict(a=nb, b=2, c=s_)
                    pview = bass.AP(cur.tensor, cur.offset + s_,
                                    [list(cur.ap[0])] + [[2 * s_, nb], [-s_, 2], [1, s_]])
                    P.op("vector", lambda e, pview=pview, shp=shp, crep_b=crep_b:
                         e.tensor_mul(
                             xm_t[:].rearrange("p (a b c) -> p a b c", **shp),
                             pview,
                             crep_b[:].rearrange("p (a b c) -> p a b c", **shp)),
                         waits=[(sGp[pb], ld[q])] + self_edge(sV), inc=sV)
                    P.op("vector", lambda e, cur=cur, nxt=nxt, crep_a=crep_a:
                         e.tensor_mul(nxt[:], cur[:], crep_a[:]),
                         waits=self_edge(sV), inc=sV)
                    P.op("vector", lambda e, nxt=nxt: e.tensor_add(nxt[:], nxt[:], xm_t[:]),
                         waits=self_edge(sV), inc=sV)
                    cur, nxt = nxt, cur
                    if st == stage_order[-1]:
                        P.op("vector", lambda e, jc=jc, cur=cur: e.tensor_copy(
                            cmat[:, jc * 128:(jc + 1) * 128], cur[:]),
                            waits=self_edge(sV), inc=sV)
                    dve_done[q] = sV.n
                    if q + 2 < len(seq):
                        issue(q + 2)
                P.barrier(ALLSEMS)

            def high_stages(cur, nxt, order, DTW, first_waits=()):
                svwt = {}
                for sti, st in enumerate(order):
                    cstride = (1 << st) // 128
                    vstage = sV.n
                    for jc in range(NJ):
                        pc = jc ^ cstride
                        wt = wtm2[jc % 2]
                        fw = list(first_waits) if (sti == 0 and jc == 0) else []
                        P.op("scalar", lambda e, jc=jc, pc=pc, st=st, cur=cur, wt=wt:
                             e.activation(
                                 wt[:, 0:DTW], cur[:, pc * DTW:(pc + 1) * DTW], AF.Copy,
                                 scale=coefO[:, st * NJ + jc:st * NJ + jc + 1]),
                             waits=fw + [(sV, svwt.get(jc - 2, vstage))]
                             + self_edge(sA), inc=sA)
                        P.op("vector", lambda e, jc=jc, st=st, cur=cur, nxt=nxt, wt=wt:
                             e.scalar_tensor_tensor(
                                 nxt[:, jc * DTW:(jc + 1) * DTW],
                                 cur[:, jc * DTW:(jc + 1) * DTW],
                                 coefS[:, st * NJ + jc:st * NJ + jc + 1],
                                 wt[:, 0:DTW], ALU.mult, ALU.add),
                             waits=[(sA, sA.n)] + self_edge(sV), inc=sV)
                        svwt[jc] = sV.n
                    svwt = {}
                    cur, nxt = nxt, cur
                return cur, nxt

            def low_stages(cur, nxt, DTW, first_waits=()):
                cv = sV.n
                for jc in range(NJ):
                    bi = jc % 2
                    bank = pbank[bi]
                    fw = list(first_waits) if jc == 0 else []
                    P.op("tensor", lambda e, jc=jc, cur=cur, bank=bank: e.matmul(
                        bank[:, 0:DTW],
                        cmat[:, jc * 128:(jc + 1) * 128],
                        cur[:, jc * DTW:(jc + 1) * DTW], start=True, stop=True),
                        waits=[(sV, cv)] + fw + bwait(bi), inc=sT)
                    P.op("vector", lambda e, jc=jc, nxt=nxt, bank=bank: e.tensor_copy(
                        nxt[:, jc * DTW:(jc + 1) * DTW],
                        bank[:, 0:DTW]),
                        waits=[(sT, sT.n)] + self_edge(sV), inc=sV)
                    bfree(bi, sV)
                return nxt, cur

            def transform(w_name, high_first, dst_hbm, do_transpose):
                NDT = D // DT
                lds = {}
                rdv = {-1: (0, 0, 0), -2: (0, 0, 0)}
                slast = [sSs[i].n for i in range(4)]

                def issue_w(dt):
                    pb = dt % 2
                    v_, t_, o_ = rdv[dt - 2]
                    w = [(sV, v_), (sT, t_)]
                    if not do_transpose and dt >= 2:
                        w.append((sSs[pb], o_))
                    P.dma("gpsimd", wjl[pb][:],
                          bass.AP(dr[w_name].tensor, dt * DT,
                                  [[D, 128], [128 * D, NJ], [1, DT]]),
                          waits=w, inc=sGp[pb])
                    lds[dt] = sGp[pb].n

                issue_w(0)
                issue_w(1)
                for dt in range(NDT):
                    pb = dt % 2
                    fw = [(sGp[pb], lds[dt]), (sT, sT.n)]
                    cur, nxt = wjl[pb], wj_b
                    if high_first:
                        cur, nxt = high_stages(cur, nxt, (11, 10, 9, 8, 7), DT,
                                               first_waits=fw)
                        cur, nxt = low_stages(cur, nxt, DT)
                    else:
                        cur, nxt = low_stages(cur, nxt, DT, first_waits=fw)
                        cur, nxt = high_stages(cur, nxt, (7, 8, 9, 10, 11), DT)
                    if do_transpose:
                        vfirst = sV.n
                        for idx, (jc, ds) in enumerate(
                                (jc, ds) for jc in range(NJ) for ds in range(DT // 128)):
                            bi = 4 + (idx % 2)
                            si = idx % 4
                            bank = pbank[bi]
                            pview = bank[:, 0:64].bitcast(BF16)
                            P.op("tensor", lambda e, jc=jc, ds=ds, cur=cur, pview=pview:
                                 e.transpose(
                                     pview,
                                     cur[:, jc * DT + ds * 128: jc * DT + ds * 128 + 128],
                                     ident[:]),
                                 waits=[(sV, vfirst)] + bwait(bi), inc=sT)
                            dchunk = (dt * DT + ds * 128) // 128
                            P.op("vector", lambda e, si=si, pview=pview:
                                 e.tensor_copy(wts[si][:], pview),
                                 waits=[(sT, sT.n), (sSs[si], slast[si])]
                                 + self_edge(sV), inc=sV)
                            bfree(bi, sV)
                            P.dma("sync",
                                  bass.AP(dst_hbm.tensor,
                                          dchunk * DH + jc * 128,
                                          [[ND * DH, 128], [1, 128]]),
                                  wts[si][:], waits=[(sV, sV.n)], inc=sSs[si])
                            slast[si] = sSs[si].n
                    else:
                        for jc in range(NJ):
                            P.dma("sync",
                                  bass.AP(dst_hbm.tensor,
                                          jc * D + dt * DT,
                                          [[NJ * D, 128], [1, DT]]),
                                  cur[:, jc * DT:(jc + 1) * DT],
                                  waits=[(sV, sV.n), (sT, sT.n)], inc=sSs[pb])
                    rdv[dt] = (sV.n, sT.n, sSs[pb].n)
                    if dt + 2 < NDT:
                        issue_w(dt + 2)
                P.barrier(ALLSEMS)

            load_coefs("U")
            build_cmats("f_repSU", "f_repOU", (0, 1, 2, 3, 4, 5, 6))
            transform("f_WexpT", False, wexp_h, True)
            P.dma("gpsimd", bvec[:], dr["f_bexp"][:], inc=sG)
            P.dma("gpsimd", fb1[:], dr["f_upbias"][:], inc=sG)
            P.dma("gpsimd", fbo[:], dr["f_bcon"][:], inc=sG)
            P.dma("gpsimd", btmp[:, 0:ND], dr["f_dbias"][:], inc=sG)
            P.barrier(ALLSEMS)
            for jc in range(NJ):
                bank = pbank[jc % 2]
                P.op("tensor", lambda e, jc=jc, bank=bank: e.matmul(
                    bank[:, 0:1], cmat[:, jc * 128:(jc + 1) * 128],
                    bvec[:, jc:jc + 1], start=True, stop=True),
                    waits=[(sV, sV.n)], inc=sT)
                P.op("vector", lambda e, jc=jc, bank=bank: e.tensor_copy(
                    bvec2[:, jc:jc + 1], bank[:, 0:1]), waits=[(sT, sT.n)], inc=sV)
            cur, nxt = bvec2, bvec
            for st in (7, 8, 9, 10, 11):
                cstride = (1 << st) // 128
                for jc in range(NJ):
                    pc = jc ^ cstride
                    P.op("vector", lambda e, jc=jc, pc=pc, st=st, cur=cur: e.tensor_scalar(
                        wtmp[:, 0:1], cur[:, pc:pc + 1],
                        coefO[:, st * NJ + jc:st * NJ + jc + 1], None, ALU.mult),
                        waits=self_edge(sV), inc=sV)
                    P.op("vector", lambda e, jc=jc, st=st, cur=cur, nxt=nxt:
                         e.scalar_tensor_tensor(
                             nxt[:, jc:jc + 1], cur[:, jc:jc + 1],
                             coefS[:, st * NJ + jc:st * NJ + jc + 1],
                             wtmp[:, 0:1], ALU.mult, ALU.add),
                         waits=self_edge(sV), inc=sV)
                cur, nxt = nxt, cur
            P.op("vector", lambda e, cur=cur: e.tensor_add(fb1[:], fb1[:], cur[:]),
                 waits=self_edge(sV), inc=sV)
            P.op("vector", lambda e: e.tensor_add(fbo[:], fbo[:], btmp[:, 0:ND]),
                 waits=self_edge(sV), inc=sV)
            P.barrier(ALLSEMS)
            load_coefs("D")
            build_cmats("f_repSD", "f_repOD", (6, 5, 4, 3, 2, 1, 0))
            transform("f_Wcon", True, wcon_h, False)
            AR.pop()

        def ln_tile(src_t, src_off, it, xt8, xsq8, xn8, rows, bcr, bcmr, xld_waits=None,
                    xsem=None, xsq_waits=()):
            t0 = it * TT
            if xsem is None:
                xsem = sG
            if xld_waits is None:
                xld_waits = [(sV, sV.n), (sT, sT.n), (sA, sA.n)]
            P.dma("gpsimd", xt8[:],
                  bass.AP(src_t, src_off + t0, [[S, 128], [128 * S, ND], [1, TT]]),
                  waits=xld_waits, inc=xsem)
            gg = xsem.n
            for dc in range(ND):
                P.op("scalar", lambda e, dc=dc: e.activation(
                    xsq8[:, dc * TT:(dc + 1) * TT], xt8[:, dc * TT:(dc + 1) * TT], AF.Square),
                    waits=(([(xsem, gg)] + list(xsq_waits)) if dc == 0 else ()), inc=sA)
            aa = sA.n
            vfree = sV.n
            for which, buf in ((0, xt8), (1, xsq8)):
                for dc in range(ND):
                    P.op("tensor", lambda e, which=which, dc=dc, buf=buf: e.matmul(
                        pbank[which][0:1, 0:TT], ones_colb[:],
                        buf[:, dc * TT:(dc + 1) * TT],
                        start=(dc == 0), stop=(dc == ND - 1)),
                        waits=(([(xsem, gg), (sV, vfree)] if which == 0 else [(sA, aa)])
                               if dc == 0 else ()),
                        inc=(sT if dc == ND - 1 else None))
            tsum = sT.n
            rowm, rowv, rowr, rowmr = rows
            P.op("vector", lambda e: e.tensor_scalar(
                rowm[:], pbank[0][0:1, 0:TT], 1.0 / D, None, ALU.mult),
                waits=[(sT, tsum)] + self_edge(sV), inc=sV)
            P.op("vector", lambda e: e.tensor_scalar(
                rowv[:], pbank[1][0:1, 0:TT], 1.0 / D, None, ALU.mult),
                waits=self_edge(sV), inc=sV)
            P.op("scalar", lambda e: e.activation(bcr[0:1, 0:TT], rowm[:], AF.Square),
                 waits=[(sV, sV.n)], inc=sA)
            P.op("vector", lambda e: e.tensor_sub(rowv[:], rowv[:], bcr[0:1, 0:TT]),
                 waits=[(sA, sA.n)] + self_edge(sV), inc=sV)
            P.op("scalar", lambda e: e.activation(rowr[:], rowv[:], AF.Ln, bias=1e-5),
                 waits=[(sV, sV.n)], inc=sA)
            P.op("scalar", lambda e: e.activation(rowr[:], rowr[:], AF.Exp, scale=-0.5),
                 waits=self_edge(sA), inc=sA)
            P.op("vector", lambda e: e.tensor_mul(rowmr[:], rowm[:], rowr[:]),
                 waits=[(sA, sA.n)] + self_edge(sV), inc=sV)
            stv, sta = sV.n, sA.n
            P.op("tensor", lambda e: e.matmul(
                pbank[2][:, 0:TT], ones_rowf, rowr[:], start=True, stop=True),
                waits=[(sA, sta), (sV, stv)], inc=sT)
            P.op("tensor", lambda e: e.matmul(
                pbank[3][:, 0:TT], ones_rowf, rowmr[:], start=True, stop=True),
                inc=sT)
            tb = sT.n
            P.op("vector", lambda e: e.tensor_copy(bcr[:], pbank[2][:, 0:TT]),
                 waits=[(sT, tb)] + self_edge(sV), inc=sV)
            P.op("vector", lambda e: e.tensor_copy(bcmr[:], pbank[3][:, 0:TT]),
                 waits=self_edge(sV), inc=sV)
            for dc in range(ND):
                P.op("vector", lambda e, dc=dc: e.tensor_mul(
                    xn8[:, dc * TT:(dc + 1) * TT], xt8[:, dc * TT:(dc + 1) * TT], bcr[:]),
                    waits=self_edge(sV), inc=sV)
                P.op("vector", lambda e, dc=dc: e.tensor_sub(
                    xn8[:, dc * TT:(dc + 1) * TT], xn8[:, dc * TT:(dc + 1) * TT], bcmr[:]),
                    waits=self_edge(sV), inc=sV)
            return tsum

        def ffn_call(b, src_ap, dst_ap, final):
            AR.push()
            wexp = AR.alloc(ND * DH)
            wcon = AR.alloc(NJ * D)
            g_t = AR.alloc(NJ * TT)
            xsq_shared = AR.alloc(ND * TT)
            bcr_shared = AR.alloc(TT)
            bcmr_shared = AR.alloc(TT)
            rowbuf_shared = AR.alloc(2 * 2 * TT, F32)
            rows_shared = (rowbuf_shared[0:1, 0:TT], rowbuf_shared[0:1, TT:2 * TT],
                           rowbuf_shared[0:1, TT:2 * TT], rowbuf_shared[0:1, 0:TT])
            ctxs = []
            for _ in range(2):
                ctxs.append(dict(
                    xt8=AR.alloc(ND * TT),
                    xsq8=xsq_shared,
                    xn8=AR.alloc(ND * TT),
                    bcr=bcr_shared,
                    bcmr=bcmr_shared,
                    rowbuf=rowbuf_shared,
                    rows=rows_shared,
                ))

            P.dma("sync", wexp[:], wexp_h[:], waits=[(sT, sT.n)], inc=sQ)
            P.dma("sync", wcon[:], wcon_h[:], inc=sQ)
            P.barrier(ALLSEMS)

            res_done = {-1: 0, -2: 0}
            h_done = {-1: 0, -2: 0}
            od = {-1: 0, -2: 0}
            stats_done = {-1: 0}
            vln = {}

            def emit_ln(it):
                cx = ctxs[it % 2]
                w = [(sV, res_done[it - 2]), (sT, h_done[it - 2]), (sO, od[it - 2])]
                xw = [(sT, stats_done[it - 1])]
                stats_done[it] = ln_tile(
                    src_ap.tensor, src_ap.offset, it, cx["xt8"], cx["xsq8"],
                    cx["xn8"], cx["rows"], cx["bcr"], cx["bcmr"], xld_waits=w,
                    xsem=sNp[it % 2], xsq_waits=xw)
                vln[it] = sV.n

            emit_ln(0)
            for it in range(NTT):
                t0 = it * TT
                cx = ctxs[it % 2]
                xn8 = cx["xn8"]
                for jc in range(NJ):
                    bi = 4 + (jc % 2)
                    bank = pbank[bi]
                    for dc in range(ND):
                        P.op("tensor", lambda e, jc=jc, dc=dc, bank=bank, xn8=xn8: e.matmul(
                            bank[:, 0:TT],
                            wexp[:, dc * DH + jc * 128: dc * DH + (jc + 1) * 128],
                            xn8[:, dc * TT:(dc + 1) * TT],
                            start=(dc == 0), stop=(dc == ND - 1)),
                            waits=(([(sV, vln[it])] + bwait(bi)) if dc == 0 else ()),
                            inc=(sT if dc == ND - 1 else None))
                    P.op("scalar", lambda e, jc=jc, bank=bank: e.activation(
                        g_t[:, jc * TT:(jc + 1) * TT], bank[:, 0:TT], AF.Gelu,
                        bias=fb1[:, jc:jc + 1]),
                        waits=[(sT, sT.n)], inc=sA)
                    bfree(bi, sA)
                h_done[it] = sT.n
                aa_g = sA.n
                if it + 1 < NTT:
                    emit_ln(it + 1)
                for oc in range(ND):
                    bi = 6 + (oc % 2)
                    bank = pbank[bi]
                    for hc in range(NJ):
                        P.op("tensor", lambda e, oc=oc, hc=hc, bank=bank: e.matmul(
                            bank[:, 0:TT],
                            wcon[:, hc * D + oc * 128: hc * D + (oc + 1) * 128],
                            g_t[:, hc * TT:(hc + 1) * TT],
                            start=(hc == 0), stop=(hc == NJ - 1)),
                            waits=(([(sA, aa_g)] + bwait(bi)) if hc == 0 else ()),
                            inc=(sT if hc == NJ - 1 else None))
                    tdone = sT.n
                    if final:
                        stage = cx["xn8"][:, 0:2 * TT].bitcast(F32)[:, 0:TT]
                    else:
                        stage = cx["xn8"][:, oc * TT:(oc + 1) * TT]
                    P.op("vector", lambda e, oc=oc, bank=bank, stage=stage, cx=cx, t0=t0:
                         e.scalar_tensor_tensor(
                             stage,
                             bank[:, 0:TT], fbo[:, oc:oc + 1],
                             cx["xt8"][:, oc * TT:(oc + 1) * TT],
                             ALU.add, ALU.add),
                         waits=[(sT, tdone), (sO, sO.n)] + self_edge(sV), inc=sV)
                    bfree(bi, sV)
                    P.dma("sync",
                          bass.AP(dst_ap.tensor, dst_ap.offset + oc * 128 * S + t0,
                                  [[S, 128], [1, TT]]),
                          stage,
                          waits=[(sV, sV.n)], inc=sO)
                res_done[it] = sV.n
                od[it] = sO.n
            P.barrier(ALLSEMS)
            AR.pop()

        def identity_call(b, src_ap, dst_ap, final):
            AR.push()
            buf = AR.alloc(ND * TT, F32 if final else BF16)
            for it in range(NTT):
                t0 = it * TT
                P.dma("gpsimd", buf[:],
                      bass.AP(src_ap.tensor, src_ap.offset + t0,
                              [[S, 128], [128 * S, ND], [1, TT]]),
                      waits=[(sO, sO.n)], inc=sG)
                for dc in range(ND):
                    P.dma("sync",
                          bass.AP(dst_ap.tensor, dst_ap.offset + dc * 128 * S + t0,
                                  [[S, 128], [1, TT]]),
                          buf[:, dc * TT:(dc + 1) * TT],
                          waits=[(sG, sG.n)], inc=sO)
            P.barrier(ALLSEMS)
            AR.pop()


        def attn_prep():
            AR.push()
            stage = AR.alloc(3 * D)
            for dc in range(ND):
                P.dma("gpsimd", stage[:],
                      bass.AP(dr["a_WqkvT"].tensor, dc * 128 * 3 * D,
                              [[3 * D, 128], [1, 3 * D]]),
                      waits=[(sQ, sQ.n)], inc=sG2)
                P.dma("sync",
                      bass.AP(wqkv_h.tensor, dc * 3 * D, [[ND * 3 * D, 128], [1, 3 * D]]),
                      stage[:], waits=[(sG2, sG2.n)], inc=sQ)
            for dc in range(ND):
                P.dma("gpsimd", stage[:, 0:D],
                      bass.AP(dr["a_WoT"].tensor, dc * 128 * D, [[D, 128], [1, D]]),
                      waits=[(sQ, sQ.n)], inc=sG2)
                P.dma("sync",
                      bass.AP(woT_h.tensor, dc * D, [[ND * D, 128], [1, D]]),
                      stage[:, 0:D], waits=[(sG2, sG2.n)], inc=sQ)
            P.barrier(ALLSEMS)
            AR.pop()

        def attn_a_call(b, src_ap, dst_ap, final):
            NQ = S // TT
            AR.push()
            ident32 = AR.alloc(256, F32)
            bvbc = AR.alloc(D)
            wq_col = AR.alloc(2, F32)[:, 0:1]
            wk_col = AR.alloc(2, F32)[:, 0:1]
            bq_c = AR.alloc(2 * ND, F32)
            bk_c = AR.alloc(2 * ND, F32)
            rsum_c = AR.alloc(4)[:, 0:2]
            sel2_c = AR.alloc(2 * 128, F32)
            bo_c = AR.alloc(2 * ND, F32)
            P.dma("sync", ident32[:], dr["c_ident"][:], waits=[(sT, sT.n)], inc=sQ)
            P.dma("gpsimd", wq_col, dr["a_wq"][:], waits=[(sV, sV.n)], inc=sG)
            P.dma("gpsimd", wk_col, dr["a_wk"][:], inc=sG)
            P.dma("gpsimd", bq_c[:], dr["a_bq"][:], inc=sG)
            P.dma("gpsimd", bk_c[:], dr["a_bk"][:], inc=sG)
            P.dma("gpsimd", rsum_c, dr["c_rsum"][:], inc=sG)
            P.dma("gpsimd", sel2_c[0:2, :], dr["c_sel2"][:], inc=sG)
            P.dma("gpsimd", bo_c[:], dr["a_bo"][:], inc=sG)
            P.dma("gpsimd", bvbc[0:1, 0:D], dr["a_bvrow"][:], inc=sG)
            gg0 = sG.n
            P.op("tensor", lambda e: e.matmul(
                pbank[0][:, 0:512],
                ones_rowb, bvbc[0:1, 0:512], start=True, stop=True),
                waits=[(sG, gg0)] + bwait(0), inc=sT)
            P.op("tensor", lambda e: e.matmul(
                pbank[1][:, 0:512],
                ones_rowb, bvbc[0:1, 512:1024], start=True, stop=True),
                waits=bwait(1), inc=sT)
            tb0 = sT.n
            P.op("vector", lambda e: e.tensor_copy(bvbc[:, 0:512], pbank[0][:, 0:512]),
                 waits=[(sT, tb0)] + self_edge(sV), inc=sV)
            P.op("vector", lambda e: e.tensor_copy(bvbc[:, 512:1024], pbank[1][:, 0:512]),
                 waits=self_edge(sV), inc=sV)
            bfree(0, sV)
            bfree(1, sV)
            P.barrier(ALLSEMS)
            qf = AR.alloc(ND * S)
            kf = AR.alloc(ND * S)
            vt = AR.alloc(NT * D)
            zf = AR.alloc(ND * S)
            UA = 3 * D * ND
            ua = AR.alloc(UA)
            xt8 = AR.alloc(ND * TT)
            xsq8 = zf[:, 0:ND * TT]
            xn8 = AR.alloc(ND * TT)
            bcr = AR.alloc(TT)
            bcmr = AR.alloc(TT)
            rowbuf = AR.alloc(2 * 2 * TT, F32)
            rows = (rowbuf[0:1, 0:TT], rowbuf[0:1, TT:2 * TT],
                    rowbuf[0:1, TT:2 * TT], rowbuf[0:1, 0:TT])
            xo32 = AR.alloc(2 * TT, F32)
            qsq = bcr
            rstd2 = xo32
            negm = AR.alloc(2 * 2 * NT, F32)

            wqkv = ua
            kaug = ua[:, 0:S]
            qaug = ua[:, S:2 * S]
            eb = ua[:, 2 * S:2 * S + NT * TT]
            vaug = ua[:, 2 * S + NT * TT:2 * S + NT * TT + NT * 72]
            wo = ua[:, 2 * S + NT * TT + NT * 72: 2 * S + NT * TT + NT * 72 + ND * D]
            mrowbuf = AR.alloc(128)

            P.dma("sync", wqkv[:], wqkv_h[:], inc=sQ)
            P.barrier(ALLSEMS)

            for it in range(NTT):
                t0 = it * TT
                ln_tile(src_ap.tensor, src_ap.offset, it, xt8, xsq8, xn8, rows, bcr, bcmr)
                vln = sV.n
                for which, dstb, bias in ((0, qf, bq_c), (1, kf, bk_c)):
                    for ec in range(ND):
                        bi = 4 + (ec % 2)
                        bank = pbank[bi]
                        for dc in range(ND):
                            P.op("tensor", lambda e, which=which, ec=ec, dc=dc, bank=bank:
                                 e.matmul(
                                     bank[:, 0:TT],
                                     wqkv[:, dc * 3 * D + which * D + ec * 128:
                                          dc * 3 * D + which * D + (ec + 1) * 128],
                                     xn8[:, dc * TT:(dc + 1) * TT],
                                     start=(dc == 0), stop=(dc == ND - 1)),
                                 waits=(([(sV, vln)] + bwait(bi)) if dc == 0 else ()),
                                 inc=(sT if dc == ND - 1 else None))
                        P.op("vector", lambda e, ec=ec, dstb=dstb, bias=bias, bank=bank, t0=t0:
                             e.tensor_scalar(
                                 dstb[:, ec * S + t0: ec * S + t0 + TT],
                                 bank[:, 0:TT], bias[:, ec:ec + 1], None, ALU.add),
                             waits=[(sT, sT.n)], inc=sV)
                        bfree(bi, sV)
                for tc in range(TT // 128):
                    tch = it * (TT // 128) + tc
                    for half in range(2):
                        bi = 6 + half
                        bank = pbank[bi]
                        for dc in range(ND):
                            P.op("tensor", lambda e, tc=tc, half=half, dc=dc, bank=bank, t0=t0:
                                 e.matmul(
                                     bank[:, 0:512],
                                     xn8[:, dc * TT + tc * 128: dc * TT + (tc + 1) * 128],
                                     wqkv[:, dc * 3 * D + 2 * D + half * 512:
                                          dc * 3 * D + 2 * D + (half + 1) * 512],
                                     start=(dc == 0), stop=(dc == ND - 1)),
                                 waits=(bwait(bi) if dc == 0 else ()),
                                 inc=(sT if dc == ND - 1 else None))
                        P.op("vector", lambda e, tch=tch, half=half, bank=bank:
                             e.scalar_tensor_tensor(
                                 vt[:, tch * D + half * 512: tch * D + (half + 1) * 512],
                                 bank[:, 0:512], 1.0,
                                 bvbc[:, half * 512:(half + 1) * 512],
                                 ALU.mult, ALU.add),
                             waits=[(sT, sT.n)], inc=sV)
                        bfree(bi, sV)
            P.barrier(ALLSEMS)

            for which, buf, wcol in ((0, qf, wq_col), (1, kf, wk_col)):
                for ec in range(ND):
                    for iq in range(NQ):
                        t0 = iq * TT
                        P.op("scalar", lambda e, buf=buf, ec=ec, t0=t0: e.activation(
                            qsq[:], buf[:, ec * S + t0: ec * S + t0 + TT], AF.Square),
                            waits=[(sV, sV.n)] + self_edge(sA), inc=sA)
                        bi = 0
                        P.op("tensor", lambda e, bank=pbank[bi]: e.matmul(
                            bank[0:2, 0:TT], rsum_c, qsq[:], start=True, stop=True),
                            waits=[(sA, sA.n)] + bwait(bi), inc=sT)
                        P.op("scalar", lambda e, bank=pbank[bi]: e.activation(
                            rstd2[0:2, :], bank[0:2, 0:TT], AF.Ln,
                            scale=1.0 / 64.0, bias=1e-6),
                            waits=[(sT, sT.n)] + self_edge(sA), inc=sA)
                        bfree(bi, sA)
                        P.op("scalar", lambda e: e.activation(
                            rstd2[0:2, :], rstd2[0:2, :], AF.Exp, scale=-0.5),
                            waits=self_edge(sA), inc=sA)
                        bi2 = 1
                        P.op("tensor", lambda e, bank=pbank[bi2]: e.matmul(
                            bank[:, 0:TT], sel2_c[0:2, :], rstd2[0:2, :],
                            start=True, stop=True),
                            waits=[(sA, sA.n)] + bwait(bi2), inc=sT)
                        P.op("vector", lambda e, buf=buf, ec=ec, t0=t0, wcol=wcol,
                             bank=pbank[bi2]: e.scalar_tensor_tensor(
                                 buf[:, ec * S + t0: ec * S + t0 + TT],
                                 buf[:, ec * S + t0: ec * S + t0 + TT],
                                 wcol, bank[:, 0:TT], ALU.mult, ALU.mult),
                             waits=[(sT, sT.n)] + self_edge(sV), inc=sV)
                        bfree(bi2, sV)
            P.barrier(ALLSEMS)

            for tc in range(NT):
                P.op("vector", lambda e, tc=tc: e.memset(
                    vaug[:, tc * 72 + 64: tc * 72 + 65], 1.0), inc=sV)
            P.barrier(ALLSEMS)

            qsqb = xt8
            qln = xn8[:, 0:2 * S].bitcast(F32)[0:1, 0:S]
            kq4 = rowbuf[0:1, TT:TT + NQ]
            km2 = rowbuf[0:1, TT + NQ:TT + NQ + 1]
            klnb = rowbuf[0:1, TT + NQ + 1:TT + NQ + 2]
            ebs = [eb, wo[:, 0:NT * TT]]
            qs2 = [qsq, bcmr]
            rst2b = [xo32[0:1, 0:TT], rowbuf[0:1, 0:TT]]
            for hp in range(8):
                ec = hp
                for half in range(2):
                    h = 2 * hp + half
                    base = half * 64
                    P.op("vector", lambda e, base=base, ec=ec: e.tensor_mul(
                        qsqb[0:64, 0:S],
                        kf[base:base + 64, ec * S:(ec + 1) * S],
                        kf[base:base + 64, ec * S:(ec + 1) * S]),
                        waits=[(sT, sT.n)] + self_edge(sV), inc=sV)
                    vsq = sV.n
                    for iq in range(NQ):
                        bi = iq % 2
                        P.op("tensor", lambda e, iq=iq, bi=bi: e.matmul(
                            pbank[bi][0:1, 0:TT], rsum_c[:, 0:1],
                            qsqb[:, iq * TT:(iq + 1) * TT],
                            start=True, stop=True),
                            waits=[(sV, vsq)] + bwait(bi), inc=sT)
                        P.op("vector", lambda e, iq=iq, bi=bi: e.reduce_max(
                            kq4[:, iq:iq + 1], pbank[bi][0:1, 0:TT], axis=AX.X),
                            waits=[(sT, sT.n)] + self_edge(sV), inc=sV)
                        bfree(bi, sV)
                    P.op("vector", lambda e: e.reduce_max(km2, kq4, axis=AX.X),
                         waits=[(sA, sA.n)] + self_edge(sV), inc=sV)
                    vkm = sV.n
                    P.op("scalar", lambda e: e.activation(klnb, km2, AF.Ln),
                         waits=[(sV, vkm)] + self_edge(sA), inc=sA)
                    P.op("scalar", lambda e: e.activation(klnb, klnb, AF.Copy,
                                                          scale=0.5, bias=-0.4),
                         waits=self_edge(sA), inc=sA)
                    P.op("vector", lambda e, base=base, ec=ec: e.tensor_mul(
                        qsqb[0:64, 0:S],
                        qf[base:base + 64, ec * S:(ec + 1) * S],
                        qf[base:base + 64, ec * S:(ec + 1) * S]),
                        waits=[(sT, sT.n)] + self_edge(sV), inc=sV)
                    vsq2 = sV.n
                    for iq in range(NQ):
                        bi = iq % 2
                        P.op("tensor", lambda e, iq=iq, bi=bi: e.matmul(
                            pbank[bi][0:1, 0:TT], rsum_c[:, 0:1],
                            qsqb[:, iq * TT:(iq + 1) * TT],
                            start=True, stop=True),
                            waits=[(sV, vsq2)] + bwait(bi), inc=sT)
                        P.op("scalar", lambda e, iq=iq, bi=bi: e.activation(
                            qln[:, iq * TT:(iq + 1) * TT], pbank[bi][0:1, 0:TT], AF.Ln),
                            waits=[(sT, sT.n)] + self_edge(sA), inc=sA)
                        bfree(bi, sA)
                    vv = sV.n
                    P.op("vector", lambda e, base=base, ec=ec: e.tensor_copy(
                        kaug[0:64, :], kf[base:base + 64, ec * S:(ec + 1) * S]),
                        waits=[(sV, vv)] + [(sT, sT.n)], inc=sV)
                    P.op("vector", lambda e, base=base, ec=ec: e.tensor_copy(
                        qaug[0:64, :], qf[base:base + 64, ec * S:(ec + 1) * S]),
                        waits=self_edge(sV), inc=sV)
                    P.op("vector", lambda e: e.memset(kaug[64:65, :], -1.0),
                         waits=self_edge(sV), inc=sV)
                    P.op("scalar", lambda e: e.activation(
                        qaug[64:65, 0:S], qln, AF.Exp, scale=0.5, bias=klnb),
                        waits=[(sT, sT.n)] + self_edge(sA), inc=sA)
                    for tc in range(NT):
                        P.op("vector", lambda e, tc=tc, h=h: e.tensor_copy(
                            vaug[:, tc * 72: tc * 72 + 64],
                            vt[:, tc * D + h * 64: tc * D + h * 64 + 64]),
                            waits=self_edge(sV), inc=sV)
                    vaugdone = sV.n
                    aaug = sA.n
                    pend = [None]

                    def znorm_flush(base=base, ec=ec):
                        t0p, bi2, qs, rs, vnorm = pend[0]
                        P.op("tensor", lambda e, bi2=bi2, rs=rs: e.matmul(
                            pbank[bi2][0:64, 0:TT], ones_rowf[:, 0:64], rs,
                            start=True, stop=True),
                            waits=[(sV, vnorm)] + bwait(bi2), inc=sT)
                        P.op("vector", lambda e, t0p=t0p, base=base, ec=ec, bi2=bi2, qs=qs:
                             e.tensor_tensor(
                                 zf[base:base + 64, ec * S + t0p: ec * S + t0p + TT],
                                 qs[0:64, 0:TT], pbank[bi2][0:64, 0:TT], ALU.mult),
                             waits=[(sT, sT.n)] + self_edge(sV), inc=sV)
                        bfree(bi2, sV)

                    for iq in range(NQ):
                        t0 = iq * TT
                        ebx = ebs[iq % 2]
                        for tc in range(NT):
                            bi = (tc % 2)
                            P.op("tensor", lambda e, tc=tc, t0=t0, bi=bi: e.matmul(
                                pbank[bi][:, 0:TT],
                                kaug[0:65, tc * 128:(tc + 1) * 128],
                                qaug[0:65, t0:t0 + TT],
                                start=True, stop=True),
                                waits=[(sA, aaug), (sV, vaugdone)] + bwait(bi), inc=sT)
                            P.op("scalar", lambda e, tc=tc, bi=bi, ebx=ebx: e.activation(
                                ebx[:, tc * TT:(tc + 1) * TT], pbank[bi][:, 0:TT], AF.Exp),
                                waits=[(sT, sT.n)] + self_edge(sA), inc=sA)
                            bfree(bi, sA)
                        aexp = sA.n
                        av_bi = 3 + 2 * (iq % 2)
                        for tc in range(NT):
                            P.op("tensor", lambda e, tc=tc, bi=av_bi, ebx=ebx: e.matmul(
                                pbank[bi][0:65, 0:TT],
                                vaug[:, tc * 72: tc * 72 + 65],
                                ebx[:, tc * TT:(tc + 1) * TT],
                                start=(tc == 0), stop=(tc == NT - 1)),
                                waits=(([(sA, aexp)] + bwait(av_bi)) if tc == 0 else ()),
                                inc=(sT if tc == NT - 1 else None))
                        tav = sT.n
                        qs = qs2[iq % 2]
                        rs = rst2b[iq % 2]
                        P.op("vector", lambda e, bi=av_bi, rs=rs: e.reciprocal(
                            rs, pbank[bi][64:65, 0:TT]),
                            waits=[(sT, tav)] + self_edge(sV), inc=sV)
                        P.op("vector", lambda e, bi=av_bi, qs=qs: e.tensor_copy(
                            qs[0:64, 0:TT], pbank[bi][0:64, 0:TT]),
                            waits=self_edge(sV), inc=sV)
                        bfree(av_bi, sV)
                        vnorm = sV.n
                        if pend[0] is not None:
                            znorm_flush()
                        pend[0] = (t0, 2 + 2 * (iq % 2), qs, rs, vnorm)
                    znorm_flush()
                    pend[0] = None
            P.barrier(ALLSEMS)

            P.dma("sync", wo[:], woT_h[:], inc=sQ)
            P.barrier(ALLSEMS)

            for iq in range(NQ):
                t0 = iq * TT
                P.dma("gpsimd", xt8[:],
                      bass.AP(src_ap.tensor, src_ap.offset + t0,
                              [[S, 128], [128 * S, ND], [1, TT]]),
                      waits=[(sV, sV.n), (sO, sO.n)], inc=sG)
                gx = sG.n
                for oc in range(ND):
                    bi = 4 + (oc % 2)
                    bank = pbank[bi]
                    for ecc in range(ND):
                        P.op("tensor", lambda e, oc=oc, ecc=ecc, bank=bank, t0=t0: e.matmul(
                            bank[:, 0:TT],
                            wo[:, ecc * D + oc * 128: ecc * D + (oc + 1) * 128],
                            zf[:, ecc * S + t0: ecc * S + t0 + TT],
                            start=(ecc == 0), stop=(ecc == ND - 1)),
                            waits=(bwait(bi) if ecc == 0 else ()),
                            inc=(sT if ecc == ND - 1 else None))
                    tdone = sT.n
                    stg = xo32 if final else xn8[:, oc * TT:(oc + 1) * TT]
                    P.op("vector", lambda e, oc=oc, bank=bank, stg=stg:
                         e.scalar_tensor_tensor(
                             stg[:, 0:TT] if stg is xo32 else stg,
                             bank[:, 0:TT], bo_c[:, oc:oc + 1],
                             xt8[:, oc * TT:(oc + 1) * TT],
                             ALU.add, ALU.add),
                         waits=[(sT, tdone), (sG, gx), (sO, sO.n)] + self_edge(sV), inc=sV)
                    bfree(bi, sV)
                    P.dma("sync",
                          bass.AP(dst_ap.tensor, dst_ap.offset + oc * 128 * S + t0,
                                  [[S, 128], [1, TT]]),
                          stg[:, 0:TT] if stg is xo32 else stg,
                          waits=[(sV, sV.n)], inc=sO)
            P.barrier(ALLSEMS)
            AR.pop()


        def attn_b_prep():
            AR.push()
            stage = AR.alloc(3 * D)
            for dc in range(ND):
                P.dma("gpsimd", stage[:],
                      bass.AP(dr["b_WqkvT"].tensor, dc * 128 * 3 * D,
                              [[3 * D, 128], [1, 3 * D]]),
                      waits=[(sQ, sQ.n)], inc=sG2)
                P.dma("sync",
                      bass.AP(wqkvB_h.tensor, dc * 3 * D, [[ND * 3 * D, 128], [1, 3 * D]]),
                      stage[:], waits=[(sG2, sG2.n)], inc=sQ)
            for dc in range(ND):
                P.dma("gpsimd", stage[:, 0:D],
                      bass.AP(dr["b_WoT"].tensor, dc * 128 * D, [[D, 128], [1, D]]),
                      waits=[(sQ, sQ.n)], inc=sG2)
                P.dma("sync",
                      bass.AP(woB_h.tensor, dc * D, [[ND * D, 128], [1, D]]),
                      stage[:, 0:D], waits=[(sG2, sG2.n)], inc=sQ)
            for (srcn, dstt, nrow, ncol) in (("b_Fc", Fc_h, S, NF2), ("b_Fs", Fs_h, S, NF2),
                                             ("b_Cr", Cr_h, NF2, S), ("b_Ci", Ci_h, NF2, S)):
                for rc in range(nrow // 128):
                    P.dma("gpsimd", stage[:, 0:ncol] if ncol <= 3 * D else None,
                          bass.AP(dr[srcn].tensor, rc * 128 * ncol, [[ncol, 128], [1, ncol]]),
                          waits=[(sQ, sQ.n)], inc=sG2)
                    P.dma("sync",
                          bass.AP(dstt.tensor, rc * 128 * ncol, [[ncol, 128], [1, ncol]]),
                          stage[:, 0:ncol], waits=[(sG2, sG2.n)], inc=sQ)
            P.barrier(ALLSEMS)
            AR.pop()
            AR.push()
            gbuf = AR.alloc(NT * D)
            fbs = [AR.alloc(2 * NT * 128) for _ in range(2)]
            gsts = [AR.alloc(2 * D) for _ in range(2)]
            kdr = AR.alloc(D)
            for tc in range(NT):
                P.dma("gpsimd", gbuf[:, tc * D:(tc + 1) * D],
                      bass.AP(dr["b_kernT"].tensor, tc * 128 * D, [[D, 128], [1, D]]),
                      inc=sG)
            P.dma("gpsimd", kdr[0:1, :], dr["b_kDrow"][:], inc=sG)
            P.barrier(ALLSEMS)
            P.op("vector", lambda e: e.tensor_add(gbuf[0:1, 0:D], gbuf[0:1, 0:D], kdr[0:1, :]),
                 inc=sV)
            P.barrier(ALLSEMS)

            def fwd_dft(rhsbuf, out_r, out_i, extra_cb=None):
                NFCp = NF2 // 128
                fld = {}
                tdn = {-1: 0, -2: 0}
                gslast = [sSs[0].n, sSs[1].n]

                def issue_f(fc):
                    pb = fc % 2
                    w = ([(sT, tdn[fc - 2])] if fc >= 2
                         else [(sV, sV.n), (sA, sA.n), (sT, sT.n)])
                    P.dma("sync", fbs[pb][:, 0:NT * 128],
                          bass.AP(Fc_h.tensor, fc * 128,
                                  [[NF2, 128], [128 * NF2, NT], [1, 128]]),
                          waits=w, inc=sFp[pb])
                    P.dma("sync", fbs[pb][:, NT * 128:2 * NT * 128],
                          bass.AP(Fs_h.tensor, fc * 128,
                                  [[NF2, 128], [128 * NF2, NT], [1, 128]]),
                          inc=sFp[pb])
                    fld[fc] = sFp[pb].n

                issue_f(0)
                issue_f(1)
                for fc in range(NFCp):
                    pb = fc % 2
                    for ri in range(2):
                        for half in range(2):
                            bi = 2 * ri + half
                            for tc in range(NT):
                                P.op("tensor", lambda e, ri=ri, half=half, tc=tc, bi=bi,
                                     pb=pb, rhsbuf=rhsbuf: e.matmul(
                                         pbank[bi][:, 0:512],
                                         fbs[pb][:, ri * NT * 128 + tc * 128:
                                                 ri * NT * 128 + (tc + 1) * 128],
                                         rhsbuf[:, tc * D + half * 512:
                                                tc * D + (half + 1) * 512],
                                         start=(tc == 0), stop=(tc == NT - 1)),
                                     waits=(([(sFp[pb], fld[fc])] + bwait(bi))
                                            if tc == 0 else ()),
                                     inc=(sT if tc == NT - 1 else None))
                    tfd = sT.n
                    tdn[fc] = tfd
                    if extra_cb is None:
                        gs = gsts[pb]
                        for ri, outt in ((0, out_r), (1, out_i)):
                            for half in range(2):
                                bi = 2 * ri + half
                                P.op("vector", lambda e, ri=ri, half=half, bi=bi, gs=gs:
                                     e.tensor_copy(gs[:, ri * D + half * 512:
                                                      ri * D + (half + 1) * 512],
                                                   pbank[bi][:, 0:512]),
                                     waits=[(sT, tfd), (sSs[pb], gslast[pb])]
                                     + self_edge(sV), inc=sV)
                                bfree(bi, sV)
                            P.dma("sync",
                                  bass.AP(outt.tensor, fc * 128 * D, [[D, 128], [1, D]]),
                                  gs[:, ri * D:(ri + 1) * D],
                                  waits=[(sV, sV.n)], inc=sSs[pb])
                        gslast[pb] = sSs[pb].n
                    else:
                        extra_cb(fc, tfd)
                    if fc + 2 < NFCp:
                        issue_f(fc + 2)

            fwd_dft(gbuf, Gr_h, Gi_h)
            P.barrier(ALLSEMS)
            AR.pop()

        def attn_b_call(src_of, dst_of, final):
            assert B == 4, "attn_b kv layout assumes B=4 per core"
            AR.push()
            wqkv = AR.alloc(ND * 3 * D)
            xsq8 = AR.alloc(ND * TT)
            bcr = AR.alloc(TT)
            bcmr = AR.alloc(TT)
            rowbuf = AR.alloc(2 * 2 * TT, F32)
            rows = (rowbuf[0:1, 0:TT], rowbuf[0:1, TT:2 * TT],
                    rowbuf[0:1, TT:2 * TT], rowbuf[0:1, 0:TT])
            cxs = [dict(xt8=AR.alloc(ND * TT), xn8=AR.alloc(ND * TT)) for _ in range(2)]
            bqB = AR.alloc(2 * ND, F32)
            bkbc = AR.alloc(D)
            bvbcB = AR.alloc(D)
            prsts = [AR.alloc(512) for _ in range(4)]
            rowtmp = AR.alloc(D)

            P.dma("sync", wqkv[:], wqkvB_h[:], waits=[(sT, sT.n)], inc=sQ)
            P.dma("gpsimd", bqB[:], dr["b_bq"][:], waits=[(sV, sV.n)], inc=sG)
            for nm, dstb in (("b_bkrow", bkbc), ("b_bvrow", bvbcB)):
                P.dma("gpsimd", rowtmp[0:1, 0:D], dr[nm][:], waits=[(sV, sV.n)], inc=sG)
                gg = sG.n
                for half in range(2):
                    P.op("tensor", lambda e, half=half: e.matmul(
                        pbank[half][:, 0:512], ones_rowb,
                        rowtmp[0:1, half * 512:(half + 1) * 512], start=True, stop=True),
                        waits=[(sG, gg)] + bwait(half), inc=sT)
                    P.op("vector", lambda e, half=half, dstb=dstb: e.tensor_copy(
                        dstb[:, half * 512:(half + 1) * 512], pbank[half][:, 0:512]),
                        waits=[(sT, sT.n)], inc=sV)
                    bfree(half, sV)
            P.barrier(ALLSEMS)

            iters = [(b, it) for b in range(B) for it in range(NTT)]
            GI = len(iters)
            proj_done = {-1: 0, -2: 0}
            stats_done = {-1: 0}
            vln = {-1: 0, -2: 0}
            slot_dma = [0, 0, 0, 0]
            slot_i = [0]

            def emit_ln1(g):
                b, it = iters[g]
                src_ap = src_of(b)
                cx = cxs[g % 2]
                w = [(sT, proj_done[g - 2]), (sV, vln[g - 2])]
                stats_done[g] = ln_tile(
                    src_ap.tensor, src_ap.offset, it, cx["xt8"], xsq8,
                    cx["xn8"], rows, bcr, bcmr, xld_waits=w,
                    xsem=sNp[g % 2], xsq_waits=[(sT, stats_done[g - 1])])
                vln[g] = sV.n

            def store_evac(psum_ap, bias_fn, dst_ap_flat):
                si = slot_i[0] % 4
                slot_i[0] += 1
                prst = prsts[si]
                P.op("vector", lambda e, prst=prst, psum_ap=psum_ap: bias_fn(e, prst),
                     waits=[(sT, sT.n), (sSs[si], slot_dma[si])] + self_edge(sV), inc=sV)
                P.dma("sync", dst_ap_flat, prst[:, 0:dst_ap_flat.ap[-1][1]],
                      waits=[(sV, sV.n)], inc=sSs[si])
                slot_dma[si] = sSs[si].n

            emit_ln1(0)
            for g in range(GI):
                b, it = iters[g]
                t0 = it * TT
                cx = cxs[g % 2]
                xn8 = cx["xn8"]
                if True:
                    for ec in range(ND):
                        bi = 4 + (ec % 2)
                        for dc in range(ND):
                            P.op("tensor", lambda e, ec=ec, dc=dc, bi=bi, xn8=xn8: e.matmul(
                                pbank[bi][:, 0:TT],
                                wqkv[:, dc * 3 * D + ec * 128: dc * 3 * D + (ec + 1) * 128],
                                xn8[:, dc * TT:(dc + 1) * TT],
                                start=(dc == 0), stop=(dc == ND - 1)),
                                waits=(([(sV, vln[g])] + bwait(bi)) if dc == 0 else ()),
                                inc=(sT if dc == ND - 1 else None))
                        bi_ = bi

                        def qbias(e, prst, bi=bi_, ec=ec):
                            return e.tensor_scalar(prst[:, 0:TT], pbank[bi][:, 0:TT],
                                                   bqB[:, ec:ec + 1], None, ALU.add)
                        store_evac(None, qbias,
                                   bass.AP(q_h.tensor, b * D * S + ec * 128 * S + t0,
                                           [[S, 128], [1, TT]]))
                        bfree(bi, sV)
                    for which, dsth, bbc in ((1, k_h, bkbc), (2, v_h, bvbcB)):
                        for tc in range(TT // 128):
                            for half in range(2):
                                bi = 4 + half
                                for dc in range(ND):
                                    P.op("tensor", lambda e, tc=tc, half=half, dc=dc,
                                         bi=bi, which=which, xn8=xn8: e.matmul(
                                             pbank[bi][:, 0:512],
                                             xn8[:, dc * TT + tc * 128: dc * TT + (tc + 1) * 128],
                                             wqkv[:, dc * 3 * D + which * D + half * 512:
                                                  dc * 3 * D + which * D + (half + 1) * 512],
                                             start=(dc == 0), stop=(dc == ND - 1)),
                                         waits=(bwait(bi) if dc == 0 else ()),
                                         inc=(sT if dc == ND - 1 else None))

                                def kvbias(e, prst, bi=bi, half=half, bbc=bbc):
                                    return e.scalar_tensor_tensor(
                                        prst[:, 0:512], pbank[bi][:, 0:512], 1.0,
                                        bbc[:, half * 512:(half + 1) * 512],
                                        ALU.mult, ALU.add)
                                store_evac(None, kvbias,
                                           bass.AP(dsth.tensor,
                                                   b * S * D + (t0 + tc * 128) * D + half * 512,
                                                   [[D, 128], [1, 512]]))
                                bfree(bi, sV)
                    proj_done[g] = sT.n
                    if g + 1 < GI:
                        emit_ln1(g + 1)
            P.barrier(ALLSEMS)
            AR.pop()
            if ATTNB_PHASE_LIMIT < 2:
                return

            AR.push()
            kbuf = AR.alloc(NT * D)
            fbufs = [AR.alloc(2 * NT * 128) for _ in range(2)]
            gchs = [AR.alloc(2 * D) for _ in range(2)]
            pch = AR.alloc(2 * D)
            t1b = AR.alloc(4 * D, F32)
            cbufs = [AR.alloc(2 * S) for _ in range(2)]
            pqbs = [AR.alloc(2 * 512) for _ in range(2)]
            kst = AR.alloc(512)
            NFC = NF2 // 128

            for b in range(B):
                for tc in range(NT):
                    P.dma("gpsimd", kbuf[:, tc * D:(tc + 1) * D],
                          bass.AP(k_h.tensor, b * S * D + tc * 128 * D, [[D, 128], [1, D]]),
                          waits=([(sT, sT.n), (sV, sV.n)] if tc == 0 else ()), inc=sG)
                P.barrier(ALLSEMS)

                fld = {}
                mmd = {}
                prd = {}
                pst = {-1: 0}
                def issue_loads(fc):
                    pb = fc % 2
                    w = []
                    if fc >= 2:
                        w = [(sT, mmd[fc - 2])]
                    P.dma("sync", fbufs[pb][:, 0:NT * 128],
                          bass.AP(Fc_h.tensor, fc * 128,
                                  [[NF2, 128], [128 * NF2, NT], [1, 128]]),
                          waits=w, inc=sFp[pb])
                    P.dma("sync", fbufs[pb][:, NT * 128:2 * NT * 128],
                          bass.AP(Fs_h.tensor, fc * 128,
                                  [[NF2, 128], [128 * NF2, NT], [1, 128]]),
                          inc=sFp[pb])
                    P.dma("gpsimd", gchs[pb][:, 0:D],
                          bass.AP(Gr_h.tensor, fc * 128 * D, [[D, 128], [1, D]]),
                          waits=(w if fc >= 2 else []) + ([(sV, prd[fc - 2])] if fc >= 2 else []),
                          inc=sGp[pb])
                    P.dma("gpsimd", gchs[pb][:, D:2 * D],
                          bass.AP(Gi_h.tensor, fc * 128 * D, [[D, 128], [1, D]]),
                          inc=sGp[pb])
                    fld[fc] = (sFp[pb].n, sGp[pb].n)

                issue_loads(0)
                issue_loads(1)
                for fc in range(NFC):
                    pb = fc % 2
                    bk = 4 * (fc % 2)
                    for ri in range(2):
                        for half in range(2):
                            bi = bk + 2 * ri + half
                            for tc in range(NT):
                                P.op("tensor", lambda e, ri=ri, half=half, tc=tc, bi=bi, pb=pb:
                                     e.matmul(
                                         pbank[bi][:, 0:512],
                                         fbufs[pb][:, ri * NT * 128 + tc * 128:
                                                   ri * NT * 128 + (tc + 1) * 128],
                                         kbuf[:, tc * D + half * 512:
                                              tc * D + (half + 1) * 512],
                                         start=(tc == 0), stop=(tc == NT - 1)),
                                     waits=(([(sFp[pb], fld[fc][0])] + bwait(bi))
                                            if tc == 0 else ()),
                                     inc=(sT if tc == NT - 1 else None))
                    mmd[fc] = sT.n
                    for half in range(2):
                        P.op("vector", lambda e, half=half, bk=bk, pb=pb: e.tensor_mul(
                            t1b[:, half * 512:(half + 1) * 512],
                            pbank[bk + half][:, 0:512],
                            gchs[pb][:, half * 512:(half + 1) * 512]),
                            waits=[(sT, mmd[fc]), (sGp[pb], fld[fc][1]), (sO2, pst[fc - 1])]
                            + self_edge(sV), inc=sV)
                        P.op("vector", lambda e, half=half, bk=bk, pb=pb: e.tensor_mul(
                            t1b[:, 1024 + half * 512:1024 + (half + 1) * 512],
                            pbank[bk + 2 + half][:, 0:512],
                            gchs[pb][:, D + half * 512: D + (half + 1) * 512]),
                            waits=self_edge(sV), inc=sV)
                        P.op("vector", lambda e, half=half: e.tensor_sub(
                            pch[:, half * 512:(half + 1) * 512],
                            t1b[:, half * 512:(half + 1) * 512],
                            t1b[:, 1024 + half * 512:1024 + (half + 1) * 512]),
                            waits=self_edge(sV), inc=sV)
                        P.op("vector", lambda e, half=half, bk=bk, pb=pb: e.tensor_mul(
                            t1b[:, half * 512:(half + 1) * 512],
                            pbank[bk + half][:, 0:512],
                            gchs[pb][:, D + half * 512: D + (half + 1) * 512]),
                            waits=self_edge(sV), inc=sV)
                        P.op("vector", lambda e, half=half, bk=bk, pb=pb: e.tensor_mul(
                            t1b[:, 1024 + half * 512:1024 + (half + 1) * 512],
                            pbank[bk + 2 + half][:, 0:512],
                            gchs[pb][:, half * 512:(half + 1) * 512]),
                            waits=self_edge(sV), inc=sV)
                        P.op("vector", lambda e, half=half: e.tensor_add(
                            pch[:, D + half * 512: D + (half + 1) * 512],
                            t1b[:, half * 512:(half + 1) * 512],
                            t1b[:, 1024 + half * 512:1024 + (half + 1) * 512]),
                            waits=self_edge(sV), inc=sV)
                    prd[fc] = sV.n
                    for bi in range(bk, bk + 4):
                        bfree(bi, sV)
                    P.dma("scalar",
                          bass.AP(Pr_h.tensor, fc * 128 * D, [[D, 128], [1, D]]),
                          pch[:, 0:D], waits=[(sV, prd[fc])], inc=sO2)
                    P.dma("scalar",
                          bass.AP(Pi_h.tensor, fc * 128 * D, [[D, 128], [1, D]]),
                          pch[:, D:2 * D], inc=sO2)
                    pst[fc] = sO2.n
                    if fc + 2 < NFC:
                        issue_loads(fc + 2)
                P.barrier(ALLSEMS)

                NTH = max(1, NT // 8)
                for eh in range(2):
                    for th in range(NTH):
                        tcs = list(range(th * 8, min(NT, th * 8 + 8)))
                        ild = {}
                        imm = {}

                        def issue_iloads(fc, eh=eh):
                            pb = fc % 2
                            w = [(sT, imm[fc - 2])] if fc >= 2 else [(sT, sT.n)]
                            P.dma("sync", cbufs[pb][:, 0:S],
                                  bass.AP(Cr_h.tensor, fc * 128 * S, [[S, 128], [1, S]]),
                                  waits=w, inc=sCp[pb])
                            P.dma("sync", cbufs[pb][:, S:2 * S],
                                  bass.AP(Ci_h.tensor, fc * 128 * S, [[S, 128], [1, S]]),
                                  inc=sCp[pb])
                            P.dma("gpsimd", pqbs[pb][:, 0:512],
                                  bass.AP(Pr_h.tensor, fc * 128 * D + eh * 512,
                                          [[D, 128], [1, 512]]),
                                  waits=w, inc=sPp[pb])
                            P.dma("gpsimd", pqbs[pb][:, 512:1024],
                                  bass.AP(Pi_h.tensor, fc * 128 * D + eh * 512,
                                          [[D, 128], [1, 512]]),
                                  inc=sPp[pb])
                            ild[fc] = (sCp[pb].n, sPp[pb].n)

                        issue_iloads(0)
                        issue_iloads(1)
                        for fc in range(NFC):
                            pb = fc % 2
                            for ti, tc in enumerate(tcs):
                                for ri in range(2):
                                    wts = []
                                    if ti == 0 and ri == 0:
                                        wts += [(sCp[pb], ild[fc][0]), (sPp[pb], ild[fc][1])]
                                    if fc == 0 and ri == 0:
                                        wts += bwait(ti)
                                    P.op("tensor", lambda e, tc=tc, ri=ri, ti=ti, fc=fc, pb=pb:
                                         e.matmul(
                                             pbank[ti][:, 0:512],
                                             cbufs[pb][:, ri * S + tc * 128:
                                                       ri * S + (tc + 1) * 128],
                                             pqbs[pb][:, ri * 512:(ri + 1) * 512],
                                             start=(fc == 0 and ri == 0),
                                             stop=(fc == NFC - 1 and ri == 1)),
                                         waits=wts,
                                         inc=(sT if (ti == len(tcs) - 1 and ri == 1) else None))
                            imm[fc] = sT.n
                            if fc + 2 < NFC:
                                issue_iloads(fc + 2)
                        tinv = sT.n
                        for ti, tc in enumerate(tcs):
                            P.op("vector", lambda e, tc=tc, ti=ti: e.tensor_copy(
                                kst[:, 0:512], pbank[ti][:, 0:512]),
                                waits=[(sT, tinv), (sO, sO.n)] + self_edge(sV), inc=sV)
                            P.dma("sync",
                                  bass.AP(ktot_h.tensor, b * S * D + tc * 128 * D + eh * 512,
                                          [[D, 128], [1, 512]]),
                                  kst[:, 0:512], waits=[(sV, sV.n)], inc=sO)
                            bfree(ti, sV)
                P.barrier(ALLSEMS)
            AR.pop()
            if ATTNB_PHASE_LIMIT < 3:
                return

            AR.push()
            kslcs = [AR.alloc(NT * 32) for _ in range(2)]
            vslcs = [AR.alloc(NT * 32) for _ in range(2)]
            kvbs = [AR.alloc(NT * 256) for _ in range(2)]
            strips = [AR.alloc(S) for _ in range(2)]
            qreps = [AR.alloc(S) for _ in range(2)]
            qyb = AR.alloc(2 * S)
            ysb = AR.alloc(512)
            drepc = AR.alloc(2 * 128, F32)
            diagm = AR.alloc(128)
            cJb = AR.alloc(16)
            stg2 = AR.alloc(2 * 2 * S)
            P.dma("gpsimd", drepc[:], dr["b_Drep"][:], inc=sG)
            P.dma("gpsimd", diagm[:], dr["c_diag"][:], inc=sG)
            P.dma("gpsimd", cJb[:], dr["c_J"][:], inc=sG)
            P.barrier(ALLSEMS)

            NQQ = S // 512 if S >= 512 else 1
            NW = min(512, S)
            gld = {}
            kv_done = {-1: 0, -2: 0}
            conv_done = {-1: 0, -2: 0}
            qy_done = {-1: 0, -2: 0}

            def p3_loads(c):
                pb = c % 2
                sl = sLp[pb]
                wk = [(sV, kv_done[c - 2])]
                for b in range(B):
                    P.dma("gpsimd",
                          bass.AP(kslcs[pb].tensor, kslcs[pb].offset + b * 8,
                                  [[kslcs[pb].ap[0][0], 128], [32, NT], [1, 8]]),
                          bass.AP(ktot_h.tensor, b * S * D + c * 8,
                                  [[D, 128], [128 * D, NT], [1, 8]]),
                          waits=(wk if b == 0 else ()), inc=sl)
                    P.dma("gpsimd",
                          bass.AP(vslcs[pb].tensor, vslcs[pb].offset + b * 8,
                                  [[vslcs[pb].ap[0][0], 128], [32, NT], [1, 8]]),
                          bass.AP(v_h.tensor, b * S * D + c * 8,
                                  [[D, 128], [128 * D, NT], [1, 8]]),
                          inc=sl)
                P.dma("gpsimd", strips[pb][:],
                      bass.AP(dr["b_strips"].tensor, c * 128 * S, [[S, 128], [1, S]]),
                      waits=[(sT, conv_done[c - 2])], inc=sl)
                for jj in range(4):
                    P.dma("gpsimd", qreps[pb][32 * jj:32 * jj + 32, :],
                          bass.AP(q_h.tensor, c * 8 * S,
                                  [[S, 8], [D * S, 4], [1, S]]),
                          waits=([(sV, qy_done[c - 2])] if jj == 0 else ()), inc=sl)
                gld[c] = sl.n

            p3_loads(0)
            for c in range(128):
                pb = c % 2
                kslc, vslc, kvb = kslcs[pb], vslcs[pb], kvbs[pb]
                stripb, qrep = strips[pb], qreps[pb]
                P.op("vector", lambda e, c=c, stripb=stripb: e.scalar_tensor_tensor(
                    stripb[:, 0:128], diagm[:], drepc[:, c:c + 1], stripb[:, 0:128],
                    ALU.mult, ALU.add),
                    waits=[(sLp[c % 2], gld[c])] + self_edge(sV), inc=sV)
                P.op("vector", lambda e: e.nop(), waits=[(sT, conv_done[c - 2])], inc=sV)
                for tc in range(NT):
                    P.op("vector", lambda e, tc=tc, kvb=kvb, kslc=kslc, vslc=vslc: e.tensor_mul(
                        kvb[:, tc * 256:(tc + 1) * 256].rearrange(
                            "p (j i b) -> p j i b", j=8, i=8, b=4),
                        bass.AP(kslc.tensor, kslc.offset + tc * 32,
                                [[kslc.ap[0][0], 128], [0, 8], [1, 8], [8, 4]]),
                        bass.AP(vslc.tensor, vslc.offset + tc * 32,
                                [[vslc.ap[0][0], 128], [1, 8], [0, 8], [8, 4]])),
                        waits=self_edge(sV), inc=sV)
                vkv = sV.n
                kv_done[c] = vkv
                if c + 1 < 128:
                    p3_loads(c + 1)
                for cc in range(2):
                    for s_ in range(NT):
                        for qq in range(max(1, (NT * 128 - 128 * s_ + 511) // 512)):
                            qqa = s_ // 4 + qq if S >= 512 else 0
                            if S >= 512:
                                t0q = max(128 * s_, 512 * qqa)
                                tend = 512 * (qqa + 1)
                            else:
                                t0q = 128 * s_
                                tend = S
                            if t0q >= tend:
                                continue
                            NWW = tend - t0q
                            bi = (qqa % 4) if S >= 512 else 0
                            P.op("tensor", lambda e, cc=cc, s_=s_, qqa=qqa, t0q=t0q,
                                 NWW=NWW, bi=bi, kvb=kvb, stripb=stripb: e.matmul(
                                     pbank[bi][:, (t0q - 512 * qqa) if S >= 512 else t0q:
                                                ((t0q - 512 * qqa) if S >= 512 else t0q) + NWW],
                                     kvb[:, s_ * 256 + cc * 128: s_ * 256 + cc * 128 + 128],
                                     stripb[:, t0q - 128 * s_: t0q - 128 * s_ + NWW],
                                     start=(s_ == 0),
                                     stop=(s_ == min(NT - 1, 4 * qqa + 3) if S >= 512
                                           else s_ == NT - 1)),
                                 waits=(([(sV, vkv)] + bwait(bi)) if s_ == 0 else ()),
                                 inc=sT)
                    tcv = sT.n
                    for qqa in range(NQQ):
                        bi = qqa % 4
                        P.op("scalar", lambda e, qqa=qqa, bi=bi: e.copy(
                            ysb[:, 0:NW], pbank[bi][:, 0:NW]),
                            waits=[(sT, tcv), (sV, sV.n)] + self_edge(sA), inc=sA)
                        P.op("vector", lambda e, cc=cc, qqa=qqa, qrep=qrep: e.tensor_mul(
                            qyb[:, cc * S + qqa * NW: cc * S + qqa * NW + NW],
                            ysb[:, 0:NW], qrep[:, qqa * NW: qqa * NW + NW]),
                            waits=[(sA, sA.n)] + self_edge(sV), inc=sV)
                        bfree(bi, sV)
                conv_done[c] = sT.n
                qy_done[c] = sV.n
                vqy = sV.n
                for cc in range(2):
                    for qqa in range(NQQ):
                        bi = 4 + (qqa % 2)
                        P.op("tensor", lambda e, cc=cc, qqa=qqa, bi=bi: e.matmul(
                            pbank[bi][0:16, 0:NW], cJb[:],
                            qyb[:, cc * S + qqa * NW: cc * S + qqa * NW + NW],
                            start=True, stop=True),
                            waits=[(sV, vqy)] + bwait(bi), inc=sT)
                        P.op("scalar", lambda e, c=c, cc=cc, qqa=qqa, bi=bi: e.copy(
                            stg2[0:16, (c % 2) * 2 * S + cc * S + qqa * NW:
                                 (c % 2) * 2 * S + cc * S + qqa * NW + NW],
                            pbank[bi][0:16, 0:NW]),
                            waits=[(sT, sT.n), (sO, sO.n)]
                            + self_edge(sA), inc=sA)
                        bfree(bi, sA)
                if c % 2 == 1:
                    vst = sA.n
                    for b in range(B):
                        for j in range(8):
                            cc, jl = j // 4, j % 4
                            m = jl * 4 + b
                            P.dma("sync",
                                  bass.AP(yb_h.tensor,
                                          b * D * S + (j * 128 + (c - 1)) * S,
                                          [[S, 2], [1, S]]),
                                  bass.AP(stg2.tensor, stg2.offset + m * stg2.ap[0][0] + cc * S,
                                          [[stg2.ap[0][0], 1], [2 * S, 2], [1, S]]),
                                  waits=[(sA, vst)], inc=sO)
            P.barrier(ALLSEMS)
            AR.pop()
            if ATTNB_PHASE_LIMIT < 4:
                return

            AR.push()
            wo = AR.alloc(ND * D)
            ybuf = AR.alloc(ND * S)
            xt8 = AR.alloc(ND * TT)
            boB = AR.alloc(2 * ND, F32)
            xo32 = AR.alloc(2 * TT, F32)
            stg3 = AR.alloc(TT)
            P.dma("sync", wo[:], woB_h[:], inc=sQ)
            P.dma("gpsimd", boB[:], dr["b_bo"][:], inc=sG)
            P.barrier(ALLSEMS)
            for b in range(B):
                src_ap = src_of(b)
                dst_ap = dst_of(b)
                P.dma("sync", ybuf[:],
                      bass.AP(yb_h.tensor, b * D * S, [[S, 128], [128 * S, ND], [1, S]]),
                      waits=[(sT, sT.n), (sO, sO.n)], inc=sQ)
                P.barrier(ALLSEMS)
                for iq in range(NTT):
                    t0 = iq * TT
                    P.dma("gpsimd", xt8[:],
                          bass.AP(src_ap.tensor, src_ap.offset + t0,
                                  [[S, 128], [128 * S, ND], [1, TT]]),
                          waits=[(sV, sV.n), (sO, sO.n)], inc=sG)
                    gx = sG.n
                    for oc in range(ND):
                        bi = 4 + (oc % 2)
                        for ecc in range(ND):
                            P.op("tensor", lambda e, oc=oc, ecc=ecc, bi=bi, t0=t0: e.matmul(
                                pbank[bi][:, 0:TT],
                                wo[:, ecc * D + oc * 128: ecc * D + (oc + 1) * 128],
                                ybuf[:, ecc * S + t0: ecc * S + t0 + TT],
                                start=(ecc == 0), stop=(ecc == ND - 1)),
                                waits=(bwait(bi) if ecc == 0 else ()),
                                inc=(sT if ecc == ND - 1 else None))
                        tdone = sT.n
                        stg = xo32 if final else stg3
                        P.op("vector", lambda e, oc=oc, bi=bi, stg=stg:
                             e.scalar_tensor_tensor(
                                 stg[:, 0:TT], pbank[bi][:, 0:TT], boB[:, oc:oc + 1],
                                 xt8[:, oc * TT:(oc + 1) * TT], ALU.add, ALU.add),
                             waits=[(sT, tdone), (sG, gx), (sO, sO.n)] + self_edge(sV), inc=sV)
                        bfree(bi, sV)
                        P.dma("sync",
                              bass.AP(dst_ap.tensor, dst_ap.offset + oc * 128 * S + t0,
                                      [[S, 128], [1, TT]]),
                              stg[:, 0:TT], waits=[(sV, sV.n)], inc=sO)
            P.barrier(ALLSEMS)
            AR.pop()

        def fwd_dft_local(rhsbuf, fbuf, cb):
            for fc in range(NF2 // 128):
                P.dma("sync", fbuf[:, 0:NT * 128],
                      bass.AP(Fc_h.tensor, fc * 128,
                              [[NF2, 128], [128 * NF2, NT], [1, 128]]),
                      waits=[(sV, sV.n), (sA, sA.n), (sT, sT.n)], inc=sQ)
                P.dma("sync", fbuf[:, NT * 128:2 * NT * 128],
                      bass.AP(Fs_h.tensor, fc * 128,
                              [[NF2, 128], [128 * NF2, NT], [1, 128]]),
                      inc=sQ)
                gq = sQ.n
                for ri in range(2):
                    for half in range(2):
                        bi = 2 * ri + half
                        for tc in range(NT):
                            P.op("tensor", lambda e, ri=ri, half=half, tc=tc, bi=bi:
                                 e.matmul(
                                     pbank[bi][:, 0:512],
                                     fbuf[:, ri * NT * 128 + tc * 128:
                                          ri * NT * 128 + (tc + 1) * 128],
                                     rhsbuf[:, tc * D + half * 512:
                                            tc * D + (half + 1) * 512],
                                     start=(tc == 0), stop=(tc == NT - 1)),
                                 waits=(([(sQ, gq)] + bwait(bi)) if tc == 0 else ()),
                                 inc=(sT if tc == NT - 1 else None))
                tfd = sT.n
                cb(fc, tfd)

        if "ffn1" in SUBS or "ffn2" in SUBS:
            ffn_prep()
        if "attn_a" in SUBS:
            attn_prep()
        if "attn_b" in SUBS:
            attn_b_prep()

        chain = [n for n in ("attn_a", "ffn1", "attn_b", "ffn2") if n in SUBS]
        xbufs = [x_res, x_res2]
        for li, name in enumerate(chain):
            final = (li == len(chain) - 1)

            def src_of(b, li=li):
                return dr["x_in"][b] if li == 0 else xbufs[(li - 1) % 2][b]

            def dst_of(b, final=final, li=li):
                return dr["y"][b] if final else xbufs[li % 2][b]

            if name.startswith("ffn"):
                for b in range(B):
                    ffn_call(b, src_of(b), dst_of(b), final)
            elif name == "attn_a":
                for b in range(B):
                    attn_a_call(b, src_of(b), dst_of(b), final)
            elif name == "attn_b":
                attn_b_call(src_of, dst_of, final)

        P.op("sync", None, waits=[(sO, sO.n), (sQ, sQ.n)])
        P.emit_all(block)

    return nc


def butterfly_coef_arrays(weights, transposed):
    NSTG, NPAIR = weights.shape[0], weights.shape[1]
    DH = 2 * NPAIR
    SELF = np.zeros((NSTG, DH), np.float32)
    OTHER = np.zeros((NSTG, DH), np.float32)
    for st in range(NSTG):
        s = 1 << st
        k = np.arange(NPAIR)
        fa = (k // s) * 2 * s + (k % s)
        fb = fa + s
        w = np.asarray(weights[st], np.float32)
        if not transposed:
            SELF[st, fa] = w[:, 0, 0]
            OTHER[st, fa] = w[:, 0, 1]
            SELF[st, fb] = w[:, 1, 1]
            OTHER[st, fb] = w[:, 1, 0]
        else:
            SELF[st, fa] = w[:, 0, 0]
            OTHER[st, fa] = w[:, 1, 0]
            SELF[st, fb] = w[:, 1, 1]
            OTHER[st, fb] = w[:, 0, 1]
    return SELF, OTHER


def rep_tiles(SELF, OTHER, NJ):
    repS = np.broadcast_to(SELF[:7].reshape(7, NJ, 1, 128), (7, NJ, 128, 128))
    repO = np.broadcast_to(OTHER[:7].reshape(7, NJ, 1, 128), (7, NJ, 128, 128))
    return (np.ascontiguousarray(repS, dtype=np.float32),
            np.ascontiguousarray(repO, dtype=np.float32))


def host_prep_core(inputs, cfg, xcore):
    d = {}
    d["x_in"] = np.ascontiguousarray(np.asarray(xcore, np.float32).transpose(0, 2, 1))
    d["c_ident"] = np.eye(128, dtype=np.float32)
    SUBS = cfg["sublayers"]
    if "attn_a" in SUBS:
        D = cfg["D"]
        d["a_WqkvT"] = np.ascontiguousarray(np.asarray(inputs["L0_attn_a_Wqkv"], np.float32).T)
        d["a_WoT"] = np.ascontiguousarray(np.asarray(inputs["L0_attn_a_Wo"], np.float32).T)
        bqkv = np.asarray(inputs["L0_attn_a_bqkv"], np.float32)
        d["a_bq"] = np.ascontiguousarray(bqkv[0:D].reshape(8, 128).T)
        d["a_bk"] = np.ascontiguousarray(bqkv[D:2 * D].reshape(8, 128).T)
        d["a_bvrow"] = np.ascontiguousarray(bqkv[2 * D:3 * D].reshape(1, D))
        d["a_bo"] = np.ascontiguousarray(
            np.asarray(inputs["L0_attn_a_bo"], np.float32).reshape(8, 128).T)
        d["a_wq"] = np.ascontiguousarray(
            np.tile(np.asarray(inputs["L0_attn_a_wq"], np.float32), 2).reshape(128, 1))
        d["a_wk"] = np.ascontiguousarray(
            np.tile(np.asarray(inputs["L0_attn_a_wk"], np.float32), 2).reshape(128, 1))
        rs = np.zeros((128, 2), np.float32)
        rs[0:64, 0] = 1.0
        rs[64:128, 1] = 1.0
        d["c_rsum"] = rs
        d["c_sel2"] = np.ascontiguousarray(rs.T)
    if "attn_b" in SUBS:
        D = cfg["D"]
        S = cfg["S"]
        NF = 2 * S
        NF2 = S + 128
        Wq = np.asarray(inputs["L0_attn_b_Wq"], np.float32)
        Wk = np.asarray(inputs["L0_attn_b_Wk"], np.float32)
        Wv = np.asarray(inputs["L0_attn_b_Wv"], np.float32)
        d["b_WqkvT"] = np.ascontiguousarray(
            np.concatenate([Wq.T, Wk.T, Wv.T], axis=1))
        d["b_WoT"] = np.ascontiguousarray(np.asarray(inputs["L0_attn_b_Wo"], np.float32).T)
        d["b_bq"] = np.ascontiguousarray(
            np.asarray(inputs["L0_attn_b_bq"], np.float32).reshape(8, 128).T)
        d["b_bkrow"] = np.asarray(inputs["L0_attn_b_bk"], np.float32).reshape(1, D)
        d["b_bvrow"] = np.asarray(inputs["L0_attn_b_bv"], np.float32).reshape(1, D)
        d["b_bo"] = np.ascontiguousarray(
            np.asarray(inputs["L0_attn_b_bo"], np.float32).reshape(8, 128).T)
        t = np.arange(S, dtype=np.float64)
        f = np.arange(NF2, dtype=np.float64)
        valid = (f <= S).astype(np.float64)
        ang = np.outer(t, f) * (2.0 * np.pi / NF)
        d["b_Fc"] = (np.cos(ang) * valid).astype(np.float32)
        d["b_Fs"] = (-np.sin(ang) * valid).astype(np.float32)
        w = np.where((f == 0) | (f == S), 1.0, 2.0) * valid
        angT = np.outer(f, t) * (2.0 * np.pi / NF)
        d["b_Cr"] = ((w[:, None] / NF) * np.cos(angT)).astype(np.float32)
        d["b_Ci"] = (-(w[:, None] / NF) * np.sin(angT)).astype(np.float32)
        d["b_kernT"] = np.ascontiguousarray(
            np.asarray(inputs["L0_attn_b_ssm_k_kernel"], np.float32).T)
        d["b_kDrow"] = np.asarray(inputs["L0_attn_b_ssm_k_D"], np.float32).reshape(1, D)
        kern = np.asarray(inputs["L0_attn_b_ssm_kernel"], np.float32)
        strips = np.zeros((128, 128, S), np.float32)
        kp = np.zeros((128, 127 + S), np.float32)
        kp[:, 127:] = kern
        for bp in range(128):
            strips[:, bp, :] = kp[:, 127 - bp:127 - bp + S]
        d["b_strips"] = strips
        d["b_Drep"] = np.ascontiguousarray(
            np.broadcast_to(np.asarray(inputs["L0_attn_b_D"], np.float32)[None, :],
                            (128, 128)))
        d["c_diag"] = np.eye(128, dtype=np.float32)
        J = np.zeros((128, 16), np.float32)
        for jl in range(4):
            for i in range(8):
                for b in range(4):
                    J[jl * 32 + i * 4 + b, jl * 4 + b] = 1.0
        d["c_J"] = J
    if "ffn1" in SUBS or "ffn2" in SUBS:
        d["f_WexpT"] = np.ascontiguousarray(np.asarray(inputs["L0_ffn_W_exp"], np.float32).T)
        d["f_Wcon"] = np.ascontiguousarray(np.asarray(inputs["L0_ffn_W_con"], np.float32))
        SU, OU = butterfly_coef_arrays(np.asarray(inputs["L0_ffn_up_weights"]), False)
        SD, OD = butterfly_coef_arrays(np.asarray(inputs["L0_ffn_down_weights"]), True)
        def coef_t(A):
            return np.ascontiguousarray(A.reshape(12, -1, 128).transpose(0, 2, 1))
        d["f_selfU"], d["f_otherU"] = coef_t(SU), coef_t(OU)
        d["f_selfD"], d["f_otherD"] = coef_t(SD), coef_t(OD)
        d["f_repSU"], d["f_repOU"] = rep_tiles(SU, OU, cfg["NJ"])
        d["f_repSD"], d["f_repOD"] = rep_tiles(SD, OD, cfg["NJ"])
        def vec_t(v, nj):
            return np.ascontiguousarray(np.asarray(v, np.float32).reshape(nj, 128).T)
        d["f_bexp"] = vec_t(inputs["L0_ffn_b_exp"], cfg["NJ"])
        d["f_upbias"] = vec_t(inputs["L0_ffn_up_bias"], cfg["NJ"])
        d["f_bcon"] = vec_t(inputs["L0_ffn_b_con"], 8)
        d["f_dbias"] = vec_t(inputs["L0_ffn_down_bias"], 8)
    return d


def kernel(**inputs):
    cfg = make_cfg()
    x = np.asarray(inputs["L0_attn_a_x"], np.float32)
    nc = build_program(cfg)
    in_maps = [host_prep_core(inputs, cfg, x[c * 4:(c + 1) * 4]) for c in range(8)]
    res = run_bass_kernel_spmd(nc, in_maps, core_ids=list(range(8)))
    outs = [np.ascontiguousarray(res.results[c]["y"].transpose(0, 2, 1)) for c in range(8)]
    return np.concatenate(outs, axis=0).astype(np.float32)
